# version 12
# baseline (speedup 1.0000x reference)
import numpy as np
import concourse.bass as bass
import concourse.mybir as mybir
from concourse.bass_utils import run_bass_kernel_spmd

F32 = mybir.dt.float32
BF16 = mybir.dt.bfloat16
AF = mybir.ActivationFunctionType
ALU = mybir.AluOpType
AX = mybir.AxisListType

MAGIC = 12582912.0
EPS = 1e-5
T = 4096
C = 1024
NT = 256
NTO = 32

ENGINES = ("pe", "act", "dve", "pool", "sp")


class _Op:
    __slots__ = ("idx", "eng", "emit", "reads", "writes", "name", "is_async",
                 "accum", "deps", "needs_inc", "event")

    def __init__(self, idx, eng, emit, reads, writes, name, is_async, accum):
        self.idx = idx
        self.eng = eng
        self.emit = emit
        self.reads = tuple(reads)
        self.writes = tuple(writes)
        self.name = name
        self.is_async = is_async
        self.accum = tuple(accum)
        self.deps = []
        self.needs_inc = False
        self.event = None


class _Sched:

    def __init__(self, nc):
        self.nc = nc
        self.ops = []
        self._has_bar = False

    def add(self, eng, emit, reads=(), writes=(), name="", is_async=False,
            accum=()):
        assert eng in ENGINES, eng
        reads = tuple(reads) + ("__bar",)
        op = _Op(len(self.ops), eng, emit, reads,
                 tuple(writes) + tuple(accum), name, is_async, accum)
        self.ops.append(op)
        return op

    def dma(self, eng, out, in_, reads=(), writes=(), name="dma"):
        def emit(q, out=out, in_=in_):
            return q.dma_start(out=out, in_=in_)
        return self.add(eng, emit, reads=reads, writes=writes, name=name,
                        is_async=True)

    def barrier(self, name="bar"):
        self._has_bar = True
        op = _Op(len(self.ops), "pool", lambda q: q.engine_nop(),
                 (), ("__bar",), name, False, ())
        self.ops.append(op)
        return op

    def _analyze(self):
        R = 16
        dma_cnt = {e: 0 for e in ENGINES}
        self._dma_gate = {}
        for op in self.ops:
            if op.is_async:
                n = dma_cnt[op.eng]
                dma_cnt[op.eng] += 1
                r = n % R
                use = n // R + 1
                key = f"dma_{op.eng}_{r}"
                op.event = (key, 16 * use)
                if use > 1:
                    self._dma_gate[op.idx] = (key, 16 * (use - 1))
        self._dma_cnt = dma_cnt
        self._R = R
        last_write = {}
        readers = {}
        accum_res = {}
        for op in self.ops:
            deps = set()
            for r in op.reads:
                j = last_write.get(r)
                if j is not None:
                    deps.add(j)
                    if accum_res.get(r) == j and self.ops[j].eng == op.eng:
                        raise RuntimeError(
                            f"{op.name} reads accum resource {r} on same "
                            f"engine as producer {self.ops[j].name}")
            for w in op.writes:
                j = last_write.get(w)
                if j is not None:
                    deps.add(j)
                deps.update(readers.get(w, ()))
            best = {}
            for j in deps:
                pj = self.ops[j]
                key = pj.event[0] if pj.is_async else ("eng", pj.eng)
                if key not in best or j > best[key]:
                    best[key] = j
            final = []
            for key, j in best.items():
                pj = self.ops[j]
                if (pj.eng == op.eng == "pe" and not pj.is_async
                        and not op.is_async):
                    continue
                final.append(j)
            op.deps = sorted(final)
            for w in op.writes:
                last_write[w] = op.idx
                readers[w] = []
                if w in op.accum:
                    accum_res[w] = op.idx
                else:
                    accum_res.pop(w, None)
            for r in op.reads:
                readers.setdefault(r, []).append(op.idx)
        for op in self.ops:
            for j in op.deps:
                if not self.ops[j].is_async:
                    self.ops[j].needs_inc = True
        eng_cnt = {e: 0 for e in ENGINES}
        for op in self.ops:
            if op.is_async:
                pass
            elif op.needs_inc:
                eng_cnt[op.eng] += 1
                op.event = ("eng_" + op.eng, eng_cnt[op.eng])
        self._eng_cnt = eng_cnt

    def build(self):
        nc = self.nc
        self._analyze()
        per_eng = {e: [o for o in self.ops if o.eng == e] for e in ENGINES}
        import contextlib
        with contextlib.ExitStack() as stack:
            sems = {}
            for e in ENGINES:
                if self._eng_cnt[e] > 0:
                    sems["eng_" + e] = stack.enter_context(
                        nc.semaphore("sem_eng_" + e))
                if self._dma_cnt[e] > 0:
                    for r in range(self._R):
                        k = f"dma_{e}_{r}"
                        sems[k] = stack.enter_context(nc.semaphore("sem_" + k))
            block = stack.enter_context(nc.Block())

            def emit_engine(q, e):
                waited = {}
                for op in per_eng[e]:
                    gate = self._dma_gate.get(op.idx)
                    if gate is not None:
                        sk, val = gate
                        if waited.get(sk, 0) < val:
                            q.wait_ge(sems[sk], val)
                            waited[sk] = val
                    for j in op.deps:
                        sk, val = self.ops[j].event
                        if waited.get(sk, 0) >= val:
                            continue
                        q.wait_ge(sems[sk], val)
                        waited[sk] = val
                    ins = op.emit(q)
                    if op.is_async:
                        ins.then_inc(sems[op.event[0]], 16)
                    elif op.needs_inc:
                        assert ins is not None, op.name
                        ins.then_inc(sems["eng_" + e], 1)

            if per_eng["sp"]:
                block.sync(lambda q: emit_engine(q, "sp"))
            if per_eng["pe"]:
                block.tensor(lambda q: emit_engine(q, "pe"))
            if per_eng["act"]:
                block.scalar(lambda q: emit_engine(q, "act"))
            if per_eng["dve"]:
                block.vector(lambda q: emit_engine(q, "dve"))
            if per_eng["pool"]:
                block.gpsimd(lambda q: emit_engine(q, "pool"))


def _build(taps=None, dbg=False):
    nc = bass.Bass("TRN2", target_bir_lowering=False)

    x_all = nc.dram_tensor("x_all", [NT * 128, C], F32, kind="ExternalInput").ap()
    W_attn = nc.dram_tensor("W_attn", [3 * C, C], F32, kind="ExternalInput").ap()
    W_proj = nc.dram_tensor("W_proj", [C, C], F32, kind="ExternalInput").ap()
    lA = nc.dram_tensor("lA", [C, 64], F32, kind="ExternalInput").ap()
    lB = nc.dram_tensor("lB", [64, 3 * C], F32, kind="ExternalInput").ap()
    pA = nc.dram_tensor("pA", [C, 64], F32, kind="ExternalInput").ap()
    pB = nc.dram_tensor("pB", [64, C], F32, kind="ExternalInput").ap()
    gamma = nc.dram_tensor("gamma", [3 * C], F32, kind="ExternalInput").ap()
    beta = nc.dram_tensor("beta", [3 * C], F32, kind="ExternalInput").ap()
    w_gate = nc.dram_tensor("w_gate", [4 * C, C], F32, kind="ExternalInput").ap()
    w_up = nc.dram_tensor("w_up", [4 * C, C], F32, kind="ExternalInput").ap()
    w_down = nc.dram_tensor("w_down", [C, 4 * C], F32, kind="ExternalInput").ap()
    y_out = nc.dram_tensor("y", [T, C], F32, kind="ExternalOutput").ap()

    _scr_kind = "ExternalOutput" if dbg else "Internal"
    lnx_scr = nc.dram_tensor("lnx_scr", [NT * 128, C], BF16, kind=_scr_kind).ap()
    xa_scr = nc.dram_tensor("xa_scr", [3 * C, T], BF16, kind=_scr_kind).ap()
    loraL = nc.dram_tensor("loraL", [C, 3 * C], F32, kind=_scr_kind).ap()
    loraL_r = loraL.rearrange("a (b c) -> (a b) c", b=3, c=C)
    xnew_scr = nc.dram_tensor("xnew_scr", [T, C], F32, kind=_scr_kind).ap()
    gq_scr = nc.dram_tensor("gq_scr", [T, 4 * C], BF16, kind=_scr_kind).ap()
    if dbg:
        dbg_wq = nc.dram_tensor("dbg_wq", [128, 8, 3072], F32, kind="ExternalOutput").ap()
        dbg_wm = nc.dram_tensor("dbg_wm", [128, 24, 1024], F32, kind="ExternalOutput").ap()
        dbg_lnfm = nc.dram_tensor("dbg_lnfm", [128, 8, 512], F32, kind="ExternalOutput").ap()
        dbg_ps = nc.dram_tensor("dbg_ps", [128, 512], F32, kind="ExternalOutput").ap()
        dbg_wq2 = nc.dram_tensor("dbg_wq2", [128, 8, 128], F32, kind="ExternalOutput").ap()
        dbg_wqB = nc.dram_tensor("dbg_wqB", [128, 8, 128], F32, kind="ExternalOutput").ap()
        dbg_S = nc.dram_tensor("dbg_S", [128, 24], F32, kind="ExternalOutput").ap()
        dbg_Q = nc.dram_tensor("dbg_Q", [128, 24], F32, kind="ExternalOutput").ap()
        dbg_uT = nc.dram_tensor("dbg_uT", [128, 8], F32, kind="ExternalOutput").ap()
        dbg_wqC = nc.dram_tensor("dbg_wqC", [128, 8, 128], F32, kind="ExternalOutput").ap()
        dbg_wqD = nc.dram_tensor("dbg_wqD", [128, 8, 128], F32, kind="ExternalOutput").ap()

        def dump_wq(dst, tag):
            for cj_ in range(8):
                s.add("dve", (lambda q, cj_=cj_: q.tensor_copy(
                    dummy1k[:, 0:128], WqkvT[:, cj_, 0:128])),
                    reads=["WqkvT"], writes=["dummy1k"], name="dq" + tag)
                s.dma("sp", dst[:, cj_, :], dummy1k[:, 0:128],
                      reads=["dummy1k"], writes=["dbgdummy"], name="dqd" + tag)
        dbg_lf2 = nc.dram_tensor("dbg_lf2", [128, 8, 512], F32, kind="ExternalOutput").ap()
        dbg_stg = nc.dram_tensor("dbg_stg", [128, 4, 512], BF16, kind="ExternalOutput").ap()
        dbg_abn = nc.dram_tensor("dbg_abn", [128, 24], F32, kind="ExternalOutput").ap()
        dbg_bbn = nc.dram_tensor("dbg_bbn", [128, 24], F32, kind="ExternalOutput").ap()
        dbg_G = nc.dram_tensor("dbg_G", [C, C], F32, kind="ExternalOutput").ap()
        dbg_u = nc.dram_tensor("dbg_u", [1, C], F32, kind="ExternalOutput").ap()

    SB_BASE = (nc.sbuf_base + 31) & ~31
    SB_TOP = nc.sbuf_top
    _names = set()

    def sb(name, shape, dtype, off):
        assert name not in _names, name
        _names.add(name)
        n = int(np.prod(shape[1:])) * mybir.dt.size(dtype)
        assert off % 32 == 0, (name, off)
        assert SB_BASE + off + n <= SB_TOP, (name, off, n)
        return nc.alloc_sbuf_tensor_at(name, list(shape), dtype,
                                       offset=SB_BASE + off).ap()

    ident_bf = sb("ident_bf", [128, 128], BF16, 0)
    ident_f32 = sb("ident_f32", [128, 128], F32, 256)
    maskc = sb("maskc", [128, 128], F32, 768)
    ones_bf = sb("ones_bf", [128, 128], BF16, 1280)
    ones_f32 = sb("ones_f32", [128, 128], F32, 1536)
    neghalf = sb("neghalf", [128, 32], F32, 2048)
    abn = sb("abn", [128, 24], F32, 2176)
    bbn = sb("bbn", [128, 24], F32, 2304)
    smalls = sb("smalls", [128, 240], F32, 2432)
    _scol = [0]

    def col(n=1):
        c0 = _scol[0]
        _scol[0] += n
        assert _scol[0] <= 240
        return smalls[:, c0:c0 + n]

    dummy1k = sb("dummy1k", [128, 1024], F32, 3392)
    negmag = sb("negmag", [128, 1], F32, 7808)
    adal = sb("adal", [128, 32], F32, 7488)
    mwcol = sb("mwcol", [128, 4], F32, 7616)
    swcol = sb("swcol", [128, 4], F32, 7776)

    Wm_tm = sb("Wm_tm", [128, 24, 1024], BF16, 8192)
    WqkvT = sb("WqkvT", [128, 8, 3072], BF16, 57344)
    G_sb = sb("G_sb", [128, 8, 1024], F32, 106496)
    u_sb = sb("u_sb", [1, 1024], F32, 139264)
    gammaF = sb("gammaF", [128, 24], F32, 143360)
    betaF = sb("betaF", [128, 24], F32, 143488)
    uT_f = sb("uT_f", [128, 8], F32, 143616)
    uT_bf = sb("uT_bf", [128, 8], BF16, 143648)
    Scol = sb("Scol", [128, 24], F32, 143680)
    Qcol = sb("Qcol", [128, 24], F32, 143808)
    meanF = sb("meanF", [128, 24], F32, 143936)
    vepsF = sb("vepsF", [128, 24], F32, 144064)
    d24 = sb("d24", [128, 24], F32, 144192)
    TBASE = 145408
    wstg = sb("wstg", [128, 2, 1024], F32, TBASE)
    lstg = sb("lstg", [128, 2, 1024], F32, TBASE + 8192)
    AT_sb = sb("AT_sb", [64, 1024], F32, TBASE + 16384)
    B_sb = sb("B_sb", [64, 3072], F32, TBASE + 20480)
    Lstg = sb("Lstg", [128, 3072], F32, TBASE + 32768)
    wsum = sb("wsum", [128, 1024], F32, TBASE + 45056)
    xin = sb("xin", [128, 8, 1024], F32, TBASE)
    lnx = sb("lnx", [128, 8, 1024], BF16, TBASE + 32768)
    lnb = sb("lnb", [128, 8, 1024], BF16, TBASE + 49152)
    G_bf = sb("G_bf", [128, 8, 1024], BF16, TBASE + 49152)
    ln1b = sb("ln1b", [128, 4, 1024], BF16, TBASE)
    lnfm = sb("lnfm", [128, 2, 8, 512], BF16, TBASE + 8192)
    stg = sb("stg", [128, 2, 4, 512], BF16, TBASE + 24576)

    Kfm = sb("Kfm", [128, 8, 4096], BF16, 8192)
    Vtm = sb("Vtm", [128, 32, 1024], BF16, 73728)
    WpT = sb("WpT", [128, 8, 1024], BF16, 139264)
    WmpT = sb("WmpT", [128, 8, 1024], BF16, 155648)
    S_f = sb("S_f", [128, 4096], F32, 172032)
    S_bf = S_f.bitcast(BF16)
    wstg2 = sb("wstg2", [128, 2, 1024], F32, 172032)
    pAT = sb("pAT", [64, 1024], F32, 180224)
    pATb = sb("pATb", [64, 1024], BF16, 184320)
    pBb = sb("pBb", [64, 1024], BF16, 186368)
    qpan = sb("qpan", [128, 8, 128], BF16, 188416)
    Pscr2 = 188416 + 2048
    vbn = sb("vbn", [128, 8, 1024], BF16, 188416)
    XT = sb("XT", [128, 1024], F32, 196608)
    qfm = sb("qfm", [128, 8, 128], BF16, 200704)
    yfm = sb("yfm", [128, 8, 128], BF16, 202752)
    zfm = sb("zfm", [128, 8, 128], BF16, 204800)
    wsb = sb("wsb", [128, 8, 128], F32, 206848)
    wtm = sb("wtm", [128, 1024], F32, 206848)
    gmaxs = sb("gmaxs", [128, 12], F32, 210944)
    lsums = sb("lsums", [128, 12], F32, 210944 + 64)
    Pscr = sb("Pscr", [128, 2, 512], BF16, 188416 + 2048)
    ytm = sb("ytm", [128, 1024], BF16, 192512)
    ztm = sb("ztm", [128, 1024], BF16, 194560)

    WgT = sb("WgT", [128, 8, 4096], BF16, 8192)
    WuT = sb("WuT", [128, 8, 4096], BF16, 73728)
    wstg4 = sb("wstg4", [128, 2, 1024], F32, 139264)
    xt4 = sb("xt4", [128, 2, 1024], F32, 147456)
    ln3x = sb("ln3x", [128, 1024], F32, 155648)
    rq4 = sb("rq4", [128, 1024], F32, 159744)
    kq = sb("kq", [128, 1024], BF16, 163840)
    kqT = sb("kqT", [128, 2, 8, 128], BF16, 165888)
    sg4 = sb("sg4", [128, 2, 512], F32, 169984)
    gu = sb("gu", [128, 4096], F32, 174080)
    gq = sb("gq", [128, 4096], BF16, 190464)
    absc = sb("absc", [128, 32], F32, 198656)
    gmax8 = sb("gmax8", [128, 8], F32, 198784)
    ms1 = sb("ms1", [1, 4], F32, 198816)

    WdT = sb("WdT", [128, 32, 1024], BF16, 8192)
    wstg4b = sb("wstg4b", [128, 2, 4096], F32, 73728)
    tern4b = sb("tern4b", [128, 4096], BF16, 106496)
    gq4 = sb("gq4", [128, 2, 4096], BF16, 114688)
    gqT = sb("gqT", [128, 2, 32, 128], BF16, 131072)
    wd4 = sb("wd4", [128, 1024], F32, 147456)
    xt4b = sb("xt4b", [128, 2, 1024], F32, 151552)
    xo4 = sb("xo4", [128, 1024], F32, 159744)

    pall = nc.alloc_psum_tensor("pall", [128, 4096], F32).ap()
    pall_bf = pall.bitcast(BF16)

    def bank(i):
        return pall[:, i * 512:(i + 1) * 512]

    def bank_bf(i):
        return pall_bf[:, i * 1024:(i + 1) * 1024]

    PB = [f"pb{i}" for i in range(8)]

    s = _Sched(nc)

    s.add("pool", lambda q: q.memset(ident_bf, 0.0), writes=["ident_bf"])
    s.add("pool", lambda q: q.affine_select(
        out=ident_bf, in_=ident_bf, compare_op=ALU.not_equal, fill=1.0, base=0,
        pattern=[[-1, 128]], channel_multiplier=1),
        reads=["ident_bf"], writes=["ident_bf"], name="id_bf")
    s.add("pool", lambda q: q.memset(ident_f32, 0.0), writes=["ident_f32"])
    s.add("pool", lambda q: q.affine_select(
        out=ident_f32, in_=ident_f32, compare_op=ALU.not_equal, fill=1.0,
        base=0, pattern=[[-1, 128]], channel_multiplier=1),
        reads=["ident_f32"], writes=["ident_f32"], name="id_f32")
    s.add("pool", lambda q: q.memset(maskc, 0.0), writes=["maskc"])
    s.add("pool", lambda q: q.affine_select(
        out=maskc, in_=maskc, compare_op=ALU.is_ge, fill=-1e30, base=0,
        pattern=[[-1, 128]], channel_multiplier=1),
        reads=["maskc"], writes=["maskc"], name="mk_mask")
    s.add("pool", lambda q: q.memset(ones_bf, 1.0), writes=["ones_bf"])
    s.add("pool", lambda q: q.memset(ones_f32, 1.0), writes=["ones_f32"])
    s.add("pool", lambda q: q.memset(neghalf, -0.5), writes=["neghalf"])
    s.add("pool", lambda q: q.memset(negmag, -MAGIC), writes=["negmag"])

    for i in range(8):
        p = i % 2
        s.dma("sp", lstg[:, p, 0:64], lA[i * 128:(i + 1) * 128, :],
              writes=[f"lstg{p}"], name=f"ldA{i}")
        s.add("pe", (lambda q, i=i, p=p: q.transpose(
            bank(6)[0:64, (i % 4) * 128:(i % 4) * 128 + 128],
            lstg[:, p, 0:64], ident_f32)),
            reads=[f"lstg{p}", "ident_f32"], writes=[PB[6]], name=f"trA{i}")
        s.add("dve", (lambda q, i=i: q.tensor_copy(
            AT_sb[:, i * 128:(i + 1) * 128],
            bank(6)[0:64, (i % 4) * 128:(i % 4) * 128 + 128])),
            reads=[PB[6]], writes=["AT_sb"], name=f"evA{i}")
    s.dma("sp", B_sb, lB, writes=["B_sb"], name="ldB")
    for r in range(8):
        for n in range(6):
            b = n % 2
            s.add("pe", (lambda q, r=r, n=n, b=b: q.matmul(
                bank(b), AT_sb[:, r * 128:(r + 1) * 128],
                B_sb[:, n * 512:(n + 1) * 512], start=True, stop=True)),
                reads=["AT_sb", "B_sb"], writes=[PB[b]], name=f"mmL{r}_{n}")
            s.add("dve", (lambda q, n=n, b=b: q.tensor_copy(
                Lstg[:, n * 512:(n + 1) * 512], bank(b))),
                reads=[PB[b]], writes=["Lstg"], name=f"evL{r}_{n}")
        s.dma("sp", loraL[r * 128:(r + 1) * 128, :], Lstg,
              reads=["Lstg"], writes=[f"loraL.{r}"], name=f"stL{r}")

    for dj in range(24):
        p = dj % 2
        s.dma("sp", wstg[:, p, :], W_attn[dj * 128:(dj + 1) * 128, :],
              writes=[f"wstg{p}"], name=f"ldWa{dj}")
        s.dma("sp", lstg[:, p, :], loraL_r[dj * 128:(dj + 1) * 128, :],
              reads=[f"loraL.{dj // 3}"], writes=[f"lstg{p}"],
              name=f"ldLr{dj}")
        s.add("dve", (lambda q, p=p, dj=dj: q.tensor_tensor(
            Wm_tm[:, dj, :], wstg[:, p, :], lstg[:, p, :], ALU.add)),
            reads=[f"wstg{p}", f"lstg{p}"], writes=["Wm_tm"], name=f"addW{dj}")
        tb = 2 + (dj % 2)
        for cj in range(8):
            s.add("pe", (lambda q, dj=dj, cj=cj, tb=tb: q.transpose(
                bank_bf(tb)[:, cj * 128:(cj + 1) * 128],
                Wm_tm[:, dj, cj * 128:(cj + 1) * 128], ident_bf)),
                reads=["Wm_tm", "ident_bf"], writes=[PB[tb]],
                name=f"trW{dj}_{cj}")
        s.add("dve", (lambda q, dj=dj, tb=tb: q.tensor_copy(
            WqkvT[:, :, dj * 128:(dj + 1) * 128],
            bank_bf(tb).rearrange("p (a b) -> p a b", a=8, b=128))),
            reads=[PB[tb]], writes=["WqkvT"], name=f"evW{dj}")

    if dbg:
        for a_ in range(8):
            for seg in range(3):
                s.add("dve", (lambda q, a_=a_, seg=seg: q.tensor_copy(
                    dummy1k, WqkvT[:, a_, seg * 1024:(seg + 1) * 1024])),
                    reads=["WqkvT"], writes=["dummy1k"], name="dbgcwq")
                s.dma("sp", dbg_wq[:, a_, seg * 1024:(seg + 1) * 1024], dummy1k,
                      reads=["dummy1k"], writes=["dbgdummy"], name="dbgwq")
        for a_ in range(24):
            s.add("dve", (lambda q, a_=a_: q.tensor_copy(
                dummy1k, Wm_tm[:, a_, :])),
                reads=["Wm_tm"], writes=["dummy1k"], name="dbgcwm")
            s.dma("sp", dbg_wm[:, a_, :], dummy1k,
                  reads=["dummy1k"], writes=["dbgdummy"], name="dbgwm")

    for nm, src, dst, c0 in (("g", gamma, gammaF, 0), ("b", beta, betaF, 32)):
        s.dma("sp", lstg[0:24, 0, 0:128], src.rearrange("(a p) -> a p", p=128),
              writes=["lstg0"], name=f"ldgb{nm}")
        s.add("pe", (lambda q, c0=c0: q.transpose(
            bank(4)[:, c0:c0 + 24], lstg[0:24, 0, 0:128],
            ident_f32[0:24, 0:24])),
            reads=["lstg0", "ident_f32"], writes=[PB[4]], name=f"trgb{nm}")
        s.add("dve", (lambda q, dst=dst, c0=c0: q.tensor_copy(
            dst, bank(4)[:, c0:c0 + 24])),
            reads=[PB[4]], writes=["gbF"], name=f"evgb{nm}")

    s.barrier("bar_1a")
    nh_col = neghalf[:, 0:1]
    st_s = [col() for _ in range(8)]
    st_q = [col() for _ in range(8)]
    st_m = [col() for _ in range(8)]
    st_r = [col() for _ in range(8)]
    st_t1 = [col() for _ in range(8)]
    st_t2 = [col() for _ in range(8)]

    def emit_ln(p, src_ap, src_res, dst_ap, dst_res):
        s.add("dve", (lambda q: q.tensor_reduce(st_s[p], src_ap, AX.X, ALU.add)),
              reads=[src_res], writes=[f"ls{p}"], name="red")
        s.add("act", (lambda q: q.activation(
            dummy1k, src_ap, AF.Square, accum_out=st_q[p])),
            reads=[src_res], writes=["dummy1k"], accum=[f"lq{p}"], name="sq")
        s.add("pool", (lambda q: q.tensor_scalar(
            st_m[p], st_s[p], 1.0 / C, None, ALU.mult)),
            reads=[f"ls{p}"], writes=[f"lm{p}"], name="m")
        s.add("pool", (lambda q: q.tensor_tensor(
            st_t1[p], st_m[p], st_m[p], ALU.mult)),
            reads=[f"lm{p}"], writes=[f"lt1{p}"], name="m2")
        s.add("pool", (lambda q: q.tensor_scalar(
            st_t2[p], st_q[p], 1.0 / C, EPS, ALU.mult, ALU.add)),
            reads=[f"lq{p}"], writes=[f"lt2{p}"], name="q2")
        s.add("pool", (lambda q: q.tensor_tensor(
            st_t2[p], st_t2[p], st_t1[p], ALU.subtract)),
            reads=[f"lt2{p}", f"lt1{p}"], writes=[f"lt2{p}"], name="ve")
        s.add("pool", (lambda q: q.tensor_tensor(
            st_r[p], st_t2[p], nh_col, ALU.pow)),
            reads=[f"lt2{p}", "neghalf"], writes=[f"lr{p}"], name="pw")
        s.add("dve", (lambda q: q.tensor_scalar(
            dst_ap, src_ap, st_m[p], st_r[p], ALU.subtract, ALU.mult)),
            reads=[src_res, f"lm{p}", f"lr{p}"], writes=[dst_res], name="nrm")

    def emit_ldx(t):
        p = t % 8
        s.dma("sp", xin[:, p, :], x_all[t * 128:(t + 1) * 128, :],
              writes=[f"R_xin{p}"], name=f"ldx{t}")

    for t in range(8):
        emit_ldx(t)
    for t in range(NT):
        p = t % 8
        emit_ln(p, xin[:, p, :], f"R_xin{p}", lnx[:, p, :], f"lnx{p}")
        for i in range(4):
            for h in range(2):
                bi = 2 * i + h
                s.add("pe", (lambda q, p=p, i=i, h=h, bi=bi, t=t: q.matmul(
                    bank(bi), lnx[:, p, i * 128:(i + 1) * 128],
                    lnx[:, p, h * 512:(h + 1) * 512],
                    start=(t == 0), stop=(t == NT - 1))),
                    reads=[f"lnx{p}"], writes=[PB[bi]], name=f"gA{t}_{bi}")
        s.dma("pool", lnx_scr[t * 128:(t + 1) * 128, :], lnx[:, p, :],
              reads=[f"lnx{p}"], writes=[f"lnxs.{t}"], name=f"spl{t}")
        if t + 8 < NT:
            emit_ldx(t + 8)

    if dbg:
        dump_wq(dbg_wqB, "B")
    for i in range(4):
        for h in range(2):
            bi = 2 * i + h
            s.add("dve", (lambda q, i=i, h=h, bi=bi: q.tensor_copy(
                G_sb[:, i, h * 512:(h + 1) * 512], bank(bi))),
                reads=[PB[bi]], writes=["G_sb"], name=f"evGA{bi}")

    for t in range(NT):
        p = t % 8
        s.dma("sp", lnb[:, p, :], lnx_scr[t * 128:(t + 1) * 128, :],
              reads=[f"lnxs.{t}"], writes=[f"lnb{p}"], name=f"ldlb{t}")
        for i in range(4):
            s.add("pe", (lambda q, p=p, i=i, t=t: q.matmul(
                bank(i), lnb[:, p, (4 + i) * 128:(5 + i) * 128],
                lnb[:, p, 512:1024],
                start=(t == 0), stop=(t == NT - 1))),
                reads=[f"lnb{p}"], writes=[PB[i]], name=f"gB{t}_{i}")
        for h in range(2):
            s.add("pe", (lambda q, p=p, h=h, t=t: q.matmul(
                bank(4 + h)[0:1, :], ones_bf[:, 0:1],
                lnb[:, p, h * 512:(h + 1) * 512],
                start=(t == 0), stop=(t == NT - 1))),
                reads=[f"lnb{p}", "ones_bf"], writes=[PB[4 + h]],
                name=f"gU{t}_{h}")

    if dbg:
        dump_wq(dbg_wqC, "C")
    for i in range(4):
        s.add("dve", (lambda q, i=i: q.tensor_copy(
            G_sb[:, 4 + i, 512:1024], bank(i))),
            reads=[PB[i]], writes=["G_sb"], name=f"evGB{i}")
    for h in range(2):
        s.add("dve", (lambda q, h=h: q.tensor_copy(
            u_sb[:, h * 512:(h + 1) * 512], bank(4 + h)[0:1, :])),
            reads=[PB[4 + h]], writes=["u_sb"], name=f"evU{h}")

    for k in range(4):
        tb = 6 + (k % 2)
        for cj in range(4):
            s.add("pe", (lambda q, k=k, cj=cj, tb=tb: q.transpose(
                bank(tb)[:, cj * 128:(cj + 1) * 128],
                G_sb[:, cj, 512 + k * 128:512 + (k + 1) * 128], ident_f32)),
                reads=["G_sb", "ident_f32"], writes=[PB[tb]],
                name=f"trG{k}_{cj}")
        s.add("dve", (lambda q, k=k, tb=tb: q.tensor_copy(
            G_sb[:, 4 + k, 0:512], bank(tb))),
            reads=[PB[tb]], writes=["G_sb"], name=f"evGT{k}")
    if dbg:
        s.dma("sp", dbg_G.rearrange("(a p) c -> p a c", p=128),
              G_sb, reads=["G_sb"], name="dbgG2")
    s.barrier("bar_st")
    for cj in range(8):
        s.add("dve", (lambda q, cj=cj: q.tensor_copy(
            G_bf[:, cj, :], G_sb[:, cj, :])),
            reads=["G_sb"], writes=["G_bf"], name=f"castG{cj}")

    for cj in range(8):
        s.add("pe", (lambda q, cj=cj: q.transpose(
            bank(5)[:, cj:cj + 1], u_sb[:, cj * 128:(cj + 1) * 128],
            ident_f32[0:1, 0:1])),
            reads=["u_sb", "ident_f32"], writes=[PB[5]], name=f"trU{cj}")
    s.add("dve", lambda q: q.tensor_copy(uT_f, bank(5)[:, 0:8]),
          reads=[PB[5]], writes=["uT_f"], name="evUT")
    s.add("dve", lambda q: q.tensor_copy(uT_bf, uT_f),
          reads=["uT_f"], writes=["uT_bf"], name="castUT")

    for dj in range(24):
        for cj in range(8):
            s.add("pe", (lambda q, dj=dj, cj=cj: q.matmul(
                bank(6)[:, dj:dj + 1],
                WqkvT[:, cj, dj * 128:(dj + 1) * 128],
                uT_bf[:, cj:cj + 1], start=(cj == 0), stop=(cj == 7))),
                reads=["WqkvT", "uT_bf"], writes=[PB[6]], name=f"mS{dj}_{cj}")
    s.add("dve", lambda q: q.tensor_copy(Scol, bank(6)[:, 0:24]),
          reads=[PB[6]], writes=["Scol"], name="evS")

    for dj in range(24):
        b0 = (dj % 2) * 2
        for cj in range(8):
            for h in range(2):
                s.add("pe", (lambda q, dj=dj, cj=cj, h=h, b0=b0: q.matmul(
                    bank(b0 + h), WqkvT[:, cj, dj * 128:(dj + 1) * 128],
                    G_bf[:, cj, h * 512:(h + 1) * 512],
                    start=(cj == 0), stop=(cj == 7))),
                    reads=["WqkvT", "G_bf"], writes=[PB[b0 + h]],
                    name=f"mM{dj}_{cj}_{h}")
        s.add("dve", (lambda q, dj=dj, b0=b0: q.tensor_tensor(
            dummy1k, pall[:, b0 * 512:(b0 + 2) * 512],
            Wm_tm[:, dj, :], ALU.mult)),
            reads=[PB[b0], PB[b0 + 1], "Wm_tm"], writes=["dummy1k"],
            name=f"qm{dj}")
        s.add("dve", (lambda q, dj=dj: q.tensor_reduce(
            Qcol[:, dj:dj + 1], dummy1k, AX.X, ALU.add)),
            reads=["dummy1k"], writes=["Qc"], name=f"qd{dj}")

    s.add("pool", lambda q: q.tensor_scalar(
        meanF, Scol, 1.0 / (NT * 128), None, ALU.mult),
        reads=["Scol"], writes=["meanF"], name="bn_m")
    s.add("pool", lambda q: q.tensor_scalar(
        vepsF, Qcol, 1.0 / (NT * 128), EPS, ALU.mult, ALU.add),
        reads=["Qc"], writes=["vepsF"], name="bn_q")
    s.add("pool", lambda q: q.tensor_tensor(d24, meanF, meanF, ALU.mult),
          reads=["meanF"], writes=["d24"], name="bn_mm")
    s.add("pool", lambda q: q.tensor_tensor(vepsF, vepsF, d24, ALU.subtract),
          reads=["vepsF", "d24"], writes=["vepsF"], name="bn_v")
    s.add("pool", lambda q: q.tensor_tensor(
        vepsF, vepsF, neghalf[:, 0:24], ALU.pow),
        reads=["vepsF", "neghalf"], writes=["vepsF"], name="bn_r")
    s.add("pool", lambda q: q.tensor_tensor(abn, vepsF, gammaF, ALU.mult),
          reads=["vepsF", "gbF"], writes=["abn"], name="bn_a")
    s.add("pool", lambda q: q.tensor_tensor(d24, meanF, abn, ALU.mult),
          reads=["meanF", "abn"], writes=["d24"], name="bn_ma")
    s.add("pool", lambda q: q.tensor_tensor(bbn, betaF, d24, ALU.subtract),
          reads=["gbF", "d24"], writes=["bbn"], name="bn_b")
    if dbg:
        s.dma("sp", dbg_abn, abn, reads=["abn"], name="dbgabn")
        s.dma("sp", dbg_bbn, bbn, reads=["bbn"], name="dbgbbn")
        s.dma("sp", dbg_u, u_sb, reads=["u_sb"], name="dbgu")
        s.dma("sp", dbg_S, Scol, reads=["Scol"], name="dbgS")
        s.dma("sp", dbg_Q, Qcol, reads=["Qc"], name="dbgQ")
        s.dma("sp", dbg_uT, uT_f, reads=["uT_f"], name="dbguT")

    if dbg:
        dump_wq(dbg_wqD, "D")
    s.barrier("bar_1b")
    def emit_ld1b(st):
        s.dma("sp", ln1b, lnx_scr[st * 512:(st + 1) * 512, :].rearrange(
            "(a p) c -> p a c", p=128),
            reads=[f"lnxs.{4 * st + i}" for i in range(4)],
            writes=["R_xin0", "R_xin1", "R_xin2", "R_xin3"], name=f"ld1b{st}")

    emit_ld1b(0)
    for st in range(8):
        lp = st % 2
        for cj in range(8):
            tb = 2 + (cj % 2)
            for i in range(4):
                s.add("pe", (lambda q, i=i, cj=cj, tb=tb: q.transpose(
                    bank_bf(tb)[:, i * 128:(i + 1) * 128],
                    ln1b[:, i, cj * 128:(cj + 1) * 128], ident_bf)),
                    reads=["R_xin0", "R_xin1", "R_xin2", "R_xin3", "ident_bf"],
                    writes=[PB[tb]], name=f"t1b{st}_{cj}_{i}")
            s.add("dve", (lambda q, cj=cj, lp=lp, tb=tb: q.tensor_copy(
                lnfm[:, lp, cj, :], bank_bf(tb)[:, 0:512])),
                reads=[PB[tb]], writes=[f"lnfm{lp}"], name=f"e1b{st}_{cj}")
        if st + 1 < 8:
            emit_ld1b(st + 1)
        if dbg and st == 0:
            for cj_ in range(8):
                s.add("dve", (lambda q, cj_=cj_: q.tensor_copy(
                    dummy1k[:, 0:512], lnfm[:, 0, cj_, :])),
                    reads=["lnfm0"], writes=["dummy1k"], name="dbgclf")
                s.dma("sp", dbg_lnfm[:, cj_, :], dummy1k[:, 0:512],
                      reads=["dummy1k"], writes=["dbgdummy"], name="dbglf")
        for g in range(6):
            sp_ = g % 2
            for djl in range(4):
                dj = g * 4 + djl
                bi = dj % 2
                for cj in range(8):
                    s.add("pe", (lambda q, dj=dj, cj=cj, bi=bi, lp=lp: q.matmul(
                        bank(bi), WqkvT[:, cj, dj * 128:(dj + 1) * 128],
                        lnfm[:, lp, cj, :], start=(cj == 0), stop=(cj == 7))),
                        reads=["WqkvT", f"lnfm{lp}"], writes=[PB[bi]],
                        name=f"qkv{st}_{dj}_{cj}")
                if dbg and st == 0 and dj == 0:
                    s.add("dve", (lambda q: q.tensor_copy(dummy1k[:, 0:512],
                                                          bank(0))),
                          reads=[PB[0]], writes=["dummy1k"], name="dbgcps")
                    s.dma("sp", dbg_ps, dummy1k[:, 0:512],
                          reads=["dummy1k"], writes=["dbgdummy"], name="dbgps")
                    for cj_ in range(8):
                        s.add("dve", (lambda q, cj_=cj_: q.tensor_copy(
                            dummy1k[:, 0:128], WqkvT[:, cj_, 0:128])),
                            reads=["WqkvT"], writes=["dummy1k"], name="dbgcw2")
                        s.dma("sp", dbg_wq2[:, cj_, :], dummy1k[:, 0:128],
                              reads=["dummy1k"], writes=["dbgdummy"], name="dbgw2")
                        s.add("dve", (lambda q, cj_=cj_: q.tensor_copy(
                            dummy1k[:, 0:512], lnfm[:, 0, cj_, :])),
                            reads=["lnfm0"], writes=["dummy1k"], name="dbgcl2")
                        s.dma("sp", dbg_lf2[:, cj_, :], dummy1k[:, 0:512],
                              reads=["dummy1k"], writes=["dbgdummy"], name="dbgl2")
                eng_e = "dve" if dj % 2 == 0 else "act"
                if eng_e == "dve":
                    s.add("dve", (lambda q, djl=djl, bi=bi, sp_=sp_:
                                  q.tensor_copy(
                        stg[:, sp_, djl, :], bank(bi))),
                        reads=[PB[bi]], writes=[f"stg{sp_}"],
                        name=f"eqkv{st}_{dj}")
                else:
                    s.add("act", (lambda q, djl=djl, bi=bi, sp_=sp_: q.copy(
                        stg[:, sp_, djl, :], bank(bi))),
                        reads=[PB[bi]], writes=[f"stg{sp_}"],
                        name=f"eqkv{st}_{dj}")
            if dbg and st == 0 and g == 0:
                s.dma("sp", dbg_stg, stg[:, 0, :, :],
                      reads=["stg0"], writes=["dbgdummy"], name="dbgstg")
            s.dma("sp", xa_scr[g * 512:(g + 1) * 512,
                               st * 512:(st + 1) * 512].rearrange(
                "(a p) t -> p a t", p=128), stg[:, sp_, :, :],
                reads=[f"stg{sp_}"], writes=[f"xa.{g}.{st}"],
                name=f"spxa{st}_{g}")

    s.barrier("bar_W2")
    for dj in range(8):
        p = dj % 2
        s.dma("sp", wstg2[:, p, :], W_proj[dj * 128:(dj + 1) * 128, :],
              writes=[f"R_S"], name=f"ldWp{dj}")
        s.add("dve", (lambda q, p=p: q.tensor_copy(
            S_bf[:, 7168:8192], wstg2[:, p, :])),
            reads=["R_S"], writes=["R_S"], name=f"cbWp{dj}")
        tb = 2 + (dj % 2)
        for cj in range(8):
            s.add("pe", (lambda q, cj=cj, tb=tb: q.transpose(
                bank_bf(tb)[:, cj * 128:(cj + 1) * 128],
                S_bf[:, 7168 + cj * 128:7168 + (cj + 1) * 128], ident_bf)),
                reads=["R_S", "ident_bf"], writes=[PB[tb]],
                name=f"trWp{dj}_{cj}")
        s.add("dve", (lambda q, dj=dj, tb=tb: q.tensor_copy(
            WpT[:, :, dj * 128:(dj + 1) * 128],
            bank_bf(tb).rearrange("p (a b) -> p a b", a=8, b=128))),
            reads=[PB[tb]], writes=["WpT"], name=f"evWp{dj}")
    for i in range(8):
        p = i % 2
        s.dma("sp", wstg2[:, p, 0:64], pA[i * 128:(i + 1) * 128, :],
              writes=["R_S"], name=f"ldpA{i}")
        s.add("pe", (lambda q, i=i, p=p: q.transpose(
            bank(6)[0:64, (i % 4) * 128:(i % 4) * 128 + 128],
            wstg2[:, p, 0:64], ident_f32)),
            reads=["R_S", "ident_f32"], writes=[PB[6]], name=f"trpA{i}")
        s.add("dve", (lambda q, i=i: q.tensor_copy(
            pAT[:, i * 128:(i + 1) * 128],
            bank(6)[0:64, (i % 4) * 128:(i % 4) * 128 + 128])),
            reads=[PB[6]], writes=["R_S"], name=f"evpA{i}")
    s.add("dve", lambda q: q.tensor_copy(pATb, pAT),
          reads=["R_S"], writes=["R_S"], name="cpATb")
    s.dma("sp", wstg2[0:64, 0, :], pB, writes=["R_S"], name="ldpB")
    s.add("dve", lambda q: q.tensor_copy(pBb, wstg2[0:64, 0, :]),
          reads=["R_S"], writes=["R_S"], name="cpBb")
    for cj in range(8):
        b0 = (cj % 2) * 2
        for h in range(2):
            s.add("pe", (lambda q, cj=cj, h=h, b0=b0: q.matmul(
                bank(b0 + h), pBb[:, cj * 128:(cj + 1) * 128],
                pATb[:, h * 512:(h + 1) * 512], start=True, stop=True)),
                reads=["R_S"], writes=[PB[b0 + h]], name=f"mmPL{cj}_{h}")
        s.add("dve", (lambda q, cj=cj, b0=b0: q.tensor_tensor(
            WmpT[:, cj, :], pall[:, b0 * 512:(b0 + 2) * 512],
            WpT[:, cj, :], ALU.add)),
            reads=[PB[b0], PB[b0 + 1], "WpT"], writes=["WmpT"],
            name=f"evPL{cj}")

    s.barrier("bar_P2")
    for cj in range(8):
        s.dma("sp", Kfm[:, cj, :], xa_scr[C + cj * 128:C + (cj + 1) * 128, :],
              reads=[f"xa.{2 + cj // 4}.{st_}" for st_ in range(8)],
              writes=["Kfm"], name=f"ldK{cj}")
        s.add("dve", (lambda q, cj=cj: q.tensor_scalar(
            Kfm[:, cj, :], Kfm[:, cj, :], abn[:, 8 + cj:9 + cj],
            bbn[:, 8 + cj:9 + cj], ALU.mult, ALU.add)),
            reads=["Kfm", "abn", "bbn"], writes=["Kfm"], name=f"bnK{cj}")
    VBN_RES = ["R_qp", "R_XT", "qfm_r", "yfm_r"]
    for sp_ in range(4):
        for cj in range(8):
            s.dma("sp", vbn[:, cj, :],
                  xa_scr[2 * C + cj * 128:2 * C + (cj + 1) * 128,
                         sp_ * 1024:(sp_ + 1) * 1024],
                  reads=[f"xa.{4 + cj // 4}.{2 * sp_}",
                         f"xa.{4 + cj // 4}.{2 * sp_ + 1}"],
                  writes=VBN_RES, name=f"ldV{sp_}_{cj}")
            s.add("dve", (lambda q, cj=cj: q.tensor_scalar(
                vbn[:, cj, :], vbn[:, cj, :], abn[:, 16 + cj:17 + cj],
                bbn[:, 16 + cj:17 + cj], ALU.mult, ALU.add)),
                reads=VBN_RES + ["abn", "bbn"], writes=VBN_RES,
                name=f"bnV{sp_}_{cj}")
        for sk in range(8):
            tb = sk % 2
            for cj in range(8):
                s.add("pe", (lambda q, sk=sk, cj=cj, tb=tb: q.transpose(
                    bank_bf(tb)[:, cj * 128:(cj + 1) * 128],
                    vbn[:, cj, sk * 128:(sk + 1) * 128], ident_bf)),
                    reads=VBN_RES + ["ident_bf"], writes=[PB[tb]],
                    name=f"trV{sp_}_{sk}_{cj}")
            if sk % 2 == 0:
                s.add("dve", (lambda q, sp_=sp_, sk=sk, tb=tb: q.tensor_copy(
                    Vtm[:, sp_ * 8 + sk, :], bank_bf(tb))),
                    reads=[PB[tb]], writes=["Vtm"], name=f"evV{sp_}_{sk}")
            else:
                s.add("act", (lambda q, sp_=sp_, sk=sk, tb=tb: q.copy(
                    Vtm[:, sp_ * 8 + sk, :], bank_bf(tb))),
                    reads=[PB[tb]], writes=["Vtm"], name=f"evV{sp_}_{sk}")

    s.barrier("bar_P3")
    c_mx = col(); c_negm = col(); c_lt = col(); c_linv = col()
    c_lnli = col(); c_b2 = col()
    c_s2 = col(); c_q2 = col(); c_m2 = col(); c_r2 = col()
    c_mr2 = col(); c_t2a = col(); c_t2b = col()

    def p3_scores_early(qt):
        nk = qt + 1
        w_all = nk * 128
        nb = (w_all + 511) // 512
        if qt == 0:
            s.dma("sp", qpan, xa_scr[0:C, 0:128].rearrange(
                "(a p) t -> p a t", p=128),
                reads=["xa.0.0", "xa.1.0"], writes=["R_qp"], name="ldq0")
        for cj in range(8):
            s.add("dve", (lambda q, cj=cj: q.tensor_scalar(
                qfm[:, cj, :], qpan[:, cj, :],
                abn[:, cj:cj + 1], bbn[:, cj:cj + 1], ALU.mult, ALU.add)),
                reads=["R_qp", "abn", "bbn"], writes=["qfm_r"],
                name=f"bnq{qt}_{cj}")
        if qt + 1 < 32:
            s.dma("sp", qpan,
                  xa_scr[0:C, (qt + 1) * 128:(qt + 2) * 128].rearrange(
                      "(a p) t -> p a t", p=128),
                  reads=[f"xa.0.{(qt + 1) // 4}", f"xa.1.{(qt + 1) // 4}"],
                  writes=["R_qp"], name=f"ldq{qt + 1}")
        for blk in range(min(nb, 2)):
            w = min(512, w_all - blk * 512)
            for cj in range(8):
                s.add("pe", (lambda q, blk=blk, w=w, cj=cj: q.matmul(
                    bank(blk)[:, 0:w], qfm[:, cj, :],
                    Kfm[:, cj, blk * 512:blk * 512 + w],
                    start=(cj == 0), stop=(cj == 7))),
                    reads=["qfm_r", "Kfm"], writes=[PB[blk]],
                    name=f"sc{qt}_{blk}_{cj}")

    def p3_scores_rest(qt):
        nk = qt + 1
        w_all = nk * 128
        nb = (w_all + 511) // 512
        blk_has_body = []
        for blk in range(nb):
            w = min(512, w_all - blk * 512)
            bi = blk % 2
            if blk >= 2:
                for cj in range(8):
                    s.add("pe", (lambda q, blk=blk, w=w, bi=bi, cj=cj:
                                 q.matmul(
                        bank(bi)[:, 0:w], qfm[:, cj, :],
                        Kfm[:, cj, blk * 512:blk * 512 + w],
                        start=(cj == 0), stop=(cj == 7))),
                        reads=["qfm_r", "Kfm"], writes=[PB[bi]],
                        name=f"sc{qt}_{blk}_{cj}")
            wd = w - 128 if blk == nb - 1 else w
            blk_has_body.append(wd > 0)
            if wd > 0:
                s.add("dve", (lambda q, blk=blk, wd=wd, bi=bi: q.tensor_scalar(
                    S_f[:, blk * 512:blk * 512 + wd], bank(bi)[:, 0:wd],
                    0.0, -3e38, ALU.add, ALU.max,
                    accum_out=gmaxs[:, blk:blk + 1])),
                    reads=[PB[bi]], writes=["R_S"], accum=[f"gmx{blk}"],
                    name=f"evs{qt}_{blk}")
            if blk == nb - 1:
                s.add("dve", (lambda q, bi=bi, wd=wd, w_all=w_all:
                              q.tensor_tensor(
                    S_f[:, w_all - 128:w_all], bank(bi)[:, wd:wd + 128],
                    maskc, ALU.add)),
                    reads=[PB[bi], "maskc"], writes=["R_S"], name=f"msk{qt}")
                s.add("dve", (lambda q, w_all=w_all, nb=nb: q.tensor_reduce(
                    gmaxs[:, nb:nb + 1], S_f[:, w_all - 128:w_all],
                    AX.X, ALU.max)),
                    reads=["R_S"], writes=["gmxd"], name=f"dmx{qt}")
        s.add("pool", (lambda q, nb=nb: q.tensor_scalar(
            c_mx, gmaxs[:, nb:nb + 1], -3e38, None, ALU.max)),
            reads=["gmxd"], writes=["c_mx"], name=f"fm0{qt}")
        for b in range(nb):
            if not blk_has_body[b]:
                continue
            s.add("pool", (lambda q, b=b: q.tensor_scalar(
                c_mx, c_mx, gmaxs[:, b:b + 1], None, ALU.max)),
                reads=["c_mx", f"gmx{b}"], writes=["c_mx"], name=f"fm{qt}_{b}")
        s.add("pool", lambda q: q.tensor_scalar(
            c_negm, c_mx, -0.03125, None, ALU.mult),
            reads=["c_mx"], writes=["c_negm"], name=f"ngm{qt}")

    def p3_soft(qt):
        nk = qt + 1
        w_all = nk * 128
        nb = (w_all + 511) // 512
        for blk in range(nb):
            w = min(512, w_all - blk * 512)
            s.add("act", (lambda q, blk=blk, w=w: q.activation(
                Pscr[:, blk % 2, 0:w], S_f[:, blk * 512:blk * 512 + w],
                AF.Exp, bias=c_negm, scale=0.03125,
                accum_out=lsums[:, blk:blk + 1])),
                reads=["R_S", "c_negm", f"Ps{blk % 2}"],
                writes=[f"Ps{blk % 2}"], accum=[f"lsm{blk}"],
                name=f"exp{qt}_{blk}")
        s.add("dve", (lambda q, nb=nb: q.tensor_reduce(
            c_lt, lsums[:, 0:nb], AX.X, ALU.add)),
            reads=[f"lsm{b}" for b in range(nb)], writes=["c_lt"],
            name=f"lt{qt}")
        s.add("dve", lambda q: q.reciprocal(c_linv, c_lt),
              reads=["c_lt"], writes=["c_linv"], name=f"li{qt}")
        s.add("act", lambda q: q.activation(
            c_lnli, c_linv, AF.Ln),
            reads=["c_linv"], writes=["c_lnli"], name=f"lnl{qt}")
        s.add("pool", lambda q: q.tensor_tensor(
            c_b2, c_negm, c_lnli, ALU.add),
            reads=["c_negm", "c_lnli"], writes=["c_b2"], name=f"b2{qt}")
        for blk in range(nb - 1, -1, -1):
            w = min(512, w_all - blk * 512)
            pp = blk % 2
            s.add("act", (lambda q, blk=blk, w=w, pp=pp: q.activation(
                Pscr[:, pp, 0:w], S_f[:, blk * 512:blk * 512 + w],
                AF.Exp, bias=c_b2, scale=0.03125)),
                reads=["R_S", "c_b2", f"Ps{pp}"], writes=[f"Ps{pp}"],
                name=f"exq{qt}_{blk}")
            nch = (w + 127) // 128
            for j in range(nch):
                s.add("pe", (lambda q, j=j, pp=pp: q.transpose(
                    bank_bf(2)[:, j * 128:(j + 1) * 128],
                    Pscr[:, pp, j * 128:(j + 1) * 128], ident_bf)),
                    reads=[f"Ps{pp}", "ident_bf"], writes=[PB[2]],
                    name=f"tp{qt}_{blk}_{j}")
            s.add("dve", (lambda q, blk=blk, nch=nch: q.tensor_copy(
                S_bf[:, 4096 + blk * 512:4096 + blk * 512 + nch * 128],
                bank_bf(2)[:, 0:nch * 128])),
                reads=[PB[2]], writes=["R_S"], name=f"etp{qt}_{blk}")
            for j in range(nch - 1, -1, -1):
                sk = blk * 4 + j
                for h in range(2):
                    s.add("pe", (lambda q, sk=sk, h=h, nk=nk: q.matmul(
                        bank(3 + h),
                        S_bf[:, 4096 + sk * 128:4096 + (sk + 1) * 128],
                        Vtm[:, sk, h * 512:(h + 1) * 512],
                        start=(sk == nk - 1), stop=(sk == 0))),
                        reads=["Vtm", "R_S"], writes=[PB[3 + h]],
                        name=f"pv{qt}_{sk}_{h}")
        s.add("dve", lambda q: q.tensor_copy(
            ytm, pall[:, 3 * 512:5 * 512]),
            reads=[PB[3], PB[4]], writes=["ytm_r"], name=f"evyt{qt}")
        for cj in range(8):
            s.add("pe", (lambda q, cj=cj: q.transpose(
                bank_bf(2)[:, cj * 128:(cj + 1) * 128],
                ytm[:, cj * 128:(cj + 1) * 128], ident_bf)),
                reads=["ytm_r", "ident_bf"], writes=[PB[2]], name=f"ty{qt}_{cj}")
        s.add("dve", lambda q: q.tensor_copy(
            yfm.rearrange("p a b -> p (a b)"), bank_bf(2)),
            reads=[PB[2]], writes=["yfm_r"], name=f"evy{qt}")
        for cj in range(8):
            for h in range(2):
                s.add("pe", (lambda q, cj=cj, h=h: q.matmul(
                    bank(5 + h), yfm[:, cj, :],
                    WpT[:, cj, h * 512:(h + 1) * 512],
                    start=(cj == 0), stop=(cj == 7))),
                    reads=["WpT", "yfm_r"], writes=[PB[5 + h]],
                    name=f"p1{qt}_{cj}_{h}")
        s.add("dve", lambda q: q.tensor_copy(ztm, pall[:, 5 * 512:7 * 512]),
              reads=[PB[5], PB[6]], writes=["ztm_r"], name=f"evzt{qt}")
        for cj in range(8):
            s.add("pe", (lambda q, cj=cj: q.transpose(
                bank_bf(2)[:, cj * 128:(cj + 1) * 128],
                ztm[:, cj * 128:(cj + 1) * 128], ident_bf)),
                reads=["ztm_r", "ident_bf"], writes=[PB[2]], name=f"tz{qt}_{cj}")
        s.add("dve", lambda q: q.tensor_copy(
            zfm.rearrange("p a b -> p (a b)"), bank_bf(2)),
            reads=[PB[2]], writes=["zfm_r"], name=f"evz{qt}")
        for cj in range(8):
            for h in range(2):
                s.add("pe", (lambda q, cj=cj, h=h: q.matmul(
                    bank(3 + h), zfm[:, cj, :],
                    WmpT[:, cj, h * 512:(h + 1) * 512],
                    start=(cj == 0), stop=(cj == 7))),
                    reads=["WmpT", "zfm_r"], writes=[PB[3 + h]],
                    name=f"p2{qt}_{cj}_{h}")
        s.add("dve", lambda q: q.tensor_copy(wtm, pall[:, 3 * 512:5 * 512]),
              reads=[PB[3], PB[4]], writes=["R_WW"], name=f"evw{qt}")

    def p3_ln2(qt):
        s.dma("sp", XT, x_all[qt * 128:(qt + 1) * 128, :],
              writes=["R_XT"], name=f"ldxt{qt}")
        s.add("dve", lambda q: q.tensor_reduce(c_s2, wtm, AX.X, ALU.add),
              reads=["R_WW"], writes=["c_s2"], name=f"l2s{qt}")
        s.add("act", lambda q: q.activation(
            dummy1k, wtm, AF.Square, accum_out=c_q2),
            reads=["R_WW"], writes=["dummy1k"], accum=["c_q2"], name=f"l2q{qt}")
        s.add("pool", lambda q: q.tensor_scalar(
            c_m2, c_s2, 1.0 / C, None, ALU.mult),
            reads=["c_s2"], writes=["c_m2"], name=f"l2m{qt}")
        s.add("pool", lambda q: q.tensor_tensor(c_t2a, c_m2, c_m2, ALU.mult),
              reads=["c_m2"], writes=["c_t2a"], name=f"l2mm{qt}")
        s.add("pool", lambda q: q.tensor_scalar(
            c_t2b, c_q2, 1.0 / C, EPS, ALU.mult, ALU.add),
            reads=["c_q2"], writes=["c_t2b"], name=f"l2qq{qt}")
        s.add("pool", lambda q: q.tensor_tensor(
            c_t2b, c_t2b, c_t2a, ALU.subtract),
            reads=["c_t2b", "c_t2a"], writes=["c_t2b"], name=f"l2v{qt}")
        s.add("pool", lambda q: q.tensor_tensor(c_r2, c_t2b, nh_col, ALU.pow),
              reads=["c_t2b", "neghalf"], writes=["c_r2"], name=f"l2r{qt}")
        s.add("pool", lambda q: q.tensor_tensor(c_mr2, c_m2, c_r2, ALU.mult),
              reads=["c_m2", "c_r2"], writes=["c_mr2"], name=f"l2mr{qt}")
        s.add("dve", lambda q: q.scalar_tensor_tensor(
            out=XT, in0=wtm, scalar=c_r2, in1=XT, op0=ALU.mult, op1=ALU.add),
            reads=["R_WW", "c_r2", "R_XT"], writes=["R_XT"], name=f"res{qt}")
        s.add("dve", lambda q: q.tensor_scalar(
            XT, XT, c_mr2, None, ALU.subtract),
            reads=["R_XT", "c_mr2"], writes=["R_XT"], name=f"res2{qt}")
        s.dma("sp", xnew_scr[qt * 128:(qt + 1) * 128, :], XT,
              reads=["R_XT"], writes=[f"xn.{qt}"], name=f"stxn{qt}")

    p3_scores_early(0)
    p3_scores_rest(0)
    for qt in range(32):
        if qt + 1 < 32:
            p3_scores_early(qt + 1)
        p3_soft(qt)
        if qt + 1 < 32:
            p3_scores_rest(qt + 1)
        p3_ln2(qt)
    s.barrier("bar_4p")

    def ffn_w_prep_1024(widx, wsrc, wdst_ap, wdst_res, tag):
        for ic in range(32):
            p = ic % 2
            s.dma("sp", wstg4[:, p, :], wsrc[ic * 128:(ic + 1) * 128, :],
                  writes=[f"wst4_{p}"], name=f"mw{tag}{ic}")
            s.add("dve", (lambda q, ic=ic, p=p: q.tensor_reduce(
                absc[:, ic:ic + 1], wstg4[:, p, :], AX.X, ALU.add,
                apply_absolute_value=True)),
                reads=[f"wst4_{p}"], writes=["absc"], name=f"ma{tag}{ic}")
        s.add("dve", lambda q: q.tensor_reduce(
            gmax8[:, 0:1], absc[:, 0:32], AX.X, ALU.add),
            reads=["absc"], writes=["gmax8"], name=f"mr{tag}")
        s.add("pe", lambda q: q.matmul(
            bank(7)[0:1, 0:1], gmax8[:, 0:1], ones_f32[:, 0:1],
            start=True, stop=True),
            reads=["gmax8", "ones_f32"], writes=[PB[7]], name=f"mm{tag}")
        s.add("dve", lambda q: q.tensor_copy(ms1[:, 0:1], bank(7)[0:1, 0:1]),
              reads=[PB[7]], writes=["ms1a"], name=f"me{tag}")
        nel = float(32 * 128 * 1024)
        s.add("pool", (lambda q, nel=nel: q.tensor_scalar(
            ms1[:, 1:2], ms1[:, 0:1], 1.0 / nel, 1e-5, ALU.mult, ALU.max)),
            reads=["ms1a"], writes=["ms1b"], name=f"mc{tag}")
        s.add("dve", lambda q: q.reciprocal(ms1[:, 2:3], ms1[:, 1:2]),
              reads=["ms1b"], writes=["ms1c"], name=f"mi{tag}")
        s.add("pool", lambda q: q.tensor_scalar(
            ms1[:, 3:4], ms1[:, 1:2], 1.0 / 127.0, None, ALU.mult),
            reads=["ms1b"], writes=["ms1d"], name=f"md{tag}")
        s.add("pe", lambda q: q.matmul(
            bank(7)[:, 0:1], ones_f32[0:1, :], ms1[:, 2:3],
            start=True, stop=True),
            reads=["ms1c", "ones_f32"], writes=[PB[7]], name=f"bc1{tag}")
        s.add("pe", lambda q: q.matmul(
            bank(7)[:, 1:2], ones_f32[0:1, :], ms1[:, 3:4],
            start=True, stop=True),
            reads=["ms1d", "ones_f32"], writes=[PB[7]], name=f"bc2{tag}")
        s.add("dve", (lambda q, widx=widx: q.tensor_copy(
            swcol[:, widx:widx + 1], bank(7)[:, 0:1])),
            reads=[PB[7]], writes=["swcol"], name=f"esw{tag}")
        s.add("dve", (lambda q, widx=widx: q.tensor_copy(
            mwcol[:, widx:widx + 1], bank(7)[:, 1:2])),
            reads=[PB[7]], writes=["mwcol"], name=f"emw{tag}")
        for ic in range(32):
            p = ic % 2
            kqb = kq if p == 0 else rq4.bitcast(BF16)[:, 0:1024]
            kqres = "kq_r" if p == 0 else "rq4"
            s.dma("sp", wstg4[:, p, :], wsrc[ic * 128:(ic + 1) * 128, :],
                  writes=[f"wst4_{p}"], name=f"tw{tag}{ic}")
            s.add("dve", (lambda q, p=p, widx=widx: q.tensor_scalar(
                wstg4[:, p, :], wstg4[:, p, :], swcol[:, widx:widx + 1],
                MAGIC, ALU.mult, ALU.add)),
                reads=[f"wst4_{p}", "swcol"], writes=[f"wst4_{p}"],
                name=f"t1{tag}{ic}")
            s.add("act", (lambda q, p=p: q.activation(
                wstg4[:, p, :], wstg4[:, p, :], AF.Identity, bias=negmag)),
                reads=[f"wst4_{p}", "negmag"], writes=[f"wst4_{p}"],
                name=f"t2{tag}{ic}")
            s.add("dve", (lambda q, p=p, kqb=kqb: q.tensor_scalar(
                kqb, wstg4[:, p, :], 1.0, -1.0, ALU.min, ALU.max)),
                reads=[f"wst4_{p}"], writes=[kqres], name=f"t3{tag}{ic}")
            tb = 4 + (ic % 2)
            for cj in range(8):
                s.add("pe", (lambda q, cj=cj, tb=tb, kqb=kqb: q.transpose(
                    bank_bf(tb)[:, cj * 128:(cj + 1) * 128],
                    kqb[:, cj * 128:(cj + 1) * 128], ident_bf)),
                    reads=[kqres, "ident_bf"], writes=[PB[tb]],
                    name=f"t4{tag}{ic}_{cj}")
            s.add("dve", (lambda q, ic=ic, tb=tb: q.tensor_copy(
                wdst_ap[:, :, ic * 128:(ic + 1) * 128],
                bank_bf(tb).rearrange("p (a b) -> p a b", a=8, b=128))),
                reads=[PB[tb]], writes=[wdst_res], name=f"t5{tag}{ic}")

    ffn_w_prep_1024(0, w_gate, WgT, "WgT", "g")
    ffn_w_prep_1024(1, w_up, WuT, "WuT", "u")

    for dc in range(8):
        s.dma("sp", gu, w_down[dc * 128:(dc + 1) * 128, :],
              writes=["gu_r"], name=f"mwd{dc}")
        s.add("dve", (lambda q, dc=dc: q.tensor_reduce(
            absc[:, dc:dc + 1], gu, AX.X, ALU.add,
            apply_absolute_value=True)),
            reads=["gu_r"], writes=["absc"], name=f"mad{dc}")
    s.add("dve", lambda q: q.tensor_reduce(
        gmax8[:, 0:1], absc[:, 0:8], AX.X, ALU.add),
        reads=["absc"], writes=["gmax8"], name="mrd")
    s.add("pe", lambda q: q.matmul(
        bank(7)[0:1, 0:1], gmax8[:, 0:1], ones_f32[:, 0:1],
        start=True, stop=True),
        reads=["gmax8", "ones_f32"], writes=[PB[7]], name="mmd")
    s.add("dve", lambda q: q.tensor_copy(ms1[:, 0:1], bank(7)[0:1, 0:1]),
          reads=[PB[7]], writes=["ms1a"], name="med")
    s.add("pool", lambda q: q.tensor_scalar(
        ms1[:, 1:2], ms1[:, 0:1], 1.0 / float(C * 4 * C), 1e-5,
        ALU.mult, ALU.max),
        reads=["ms1a"], writes=["ms1b"], name="mcd")
    s.add("dve", lambda q: q.reciprocal(ms1[:, 2:3], ms1[:, 1:2]),
          reads=["ms1b"], writes=["ms1c"], name="mid")
    s.add("pool", lambda q: q.tensor_scalar(
        ms1[:, 3:4], ms1[:, 1:2], 1.0 / 127.0, None, ALU.mult),
        reads=["ms1b"], writes=["ms1d"], name="mdd")
    s.add("pe", lambda q: q.matmul(
        bank(7)[:, 0:1], ones_f32[0:1, :], ms1[:, 2:3], start=True, stop=True),
        reads=["ms1c", "ones_f32"], writes=[PB[7]], name="bc1d")
    s.add("pe", lambda q: q.matmul(
        bank(7)[:, 1:2], ones_f32[0:1, :], ms1[:, 3:4], start=True, stop=True),
        reads=["ms1d", "ones_f32"], writes=[PB[7]], name="bc2d")
    s.add("dve", lambda q: q.tensor_copy(swcol[:, 2:3], bank(7)[:, 0:1]),
          reads=[PB[7]], writes=["swcol"], name="eswd")
    s.add("dve", lambda q: q.tensor_copy(mwcol[:, 2:3], bank(7)[:, 1:2]),
          reads=[PB[7]], writes=["mwcol"], name="emwd")

    c3 = {k: [col(), col()] for k in ("s", "q", "m", "r", "am", "ami", "qs",
                                      "ag", "au", "am2", "am2c", "qs2",
                                      "t1", "t2", "t3")}

    def chain4a(t):
        p = t % 2
        s.dma("sp", xt4[:, p, :], xnew_scr[t * 128:(t + 1) * 128, :],
              reads=[f"xn.{t}"], writes=[f"xt4{p}"], name=f"ldx4{t}")
        s.add("dve", (lambda q, p=p: q.tensor_reduce(
            c3["s"][p], xt4[:, p, :], AX.X, ALU.add)),
            reads=[f"xt4{p}"], writes=[f"c3s{p}"], name=f"l3s{t}")
        s.add("act", (lambda q, p=p: q.activation(
            dummy1k, xt4[:, p, :], AF.Square, accum_out=c3["q"][p])),
            reads=[f"xt4{p}"], writes=["dummy1k"], accum=[f"c3q{p}"],
            name=f"l3q{t}")
        s.add("pool", (lambda q, p=p: q.tensor_scalar(
            c3["m"][p], c3["s"][p], 1.0 / C, None, ALU.mult)),
            reads=[f"c3s{p}"], writes=[f"c3m{p}"], name=f"l3m{t}")
        s.add("pool", (lambda q, p=p: q.tensor_tensor(
            c3["t1"][p], c3["m"][p], c3["m"][p], ALU.mult)),
            reads=[f"c3m{p}"], writes=[f"c3t1{p}"], name=f"l3mm{t}")
        s.add("pool", (lambda q, p=p: q.tensor_scalar(
            c3["t2"][p], c3["q"][p], 1.0 / C, EPS, ALU.mult, ALU.add)),
            reads=[f"c3q{p}"], writes=[f"c3t2{p}"], name=f"l3qq{t}")
        s.add("pool", (lambda q, p=p: q.tensor_tensor(
            c3["t2"][p], c3["t2"][p], c3["t1"][p], ALU.subtract)),
            reads=[f"c3t2{p}", f"c3t1{p}"], writes=[f"c3t2{p}"], name=f"l3v{t}")
        s.add("pool", (lambda q, p=p: q.tensor_tensor(
            c3["r"][p], c3["t2"][p], nh_col, ALU.pow)),
            reads=[f"c3t2{p}", "neghalf"], writes=[f"c3r{p}"], name=f"l3r{t}")
        s.add("dve", (lambda q, p=p: q.tensor_scalar(
            ln3x, xt4[:, p, :], c3["m"][p], c3["r"][p], ALU.subtract,
            ALU.mult)),
            reads=[f"xt4{p}", f"c3m{p}", f"c3r{p}"], writes=["ln3x"],
            name=f"l3n{t}")
        s.add("dve", (lambda q, p=p: q.tensor_reduce(
            c3["am"][p], ln3x, AX.X, ALU.max, apply_absolute_value=True)),
            reads=["ln3x"], writes=[f"c3am{p}"], name=f"am{t}")
        s.add("dve", (lambda q, p=p: q.tensor_scalar(
            c3["am"][p], c3["am"][p], 1e-5, None, ALU.max)),
            reads=[f"c3am{p}"], writes=[f"c3am{p}"], name=f"amc{t}")
        s.add("dve", (lambda q, p=p: q.reciprocal(c3["ami"][p], c3["am"][p])),
              reads=[f"c3am{p}"], writes=[f"c3ami{p}"], name=f"ami{t}")
        s.add("dve", (lambda q, p=p: q.tensor_scalar(
            c3["qs"][p], c3["ami"][p], 127.0, None, ALU.mult)),
            reads=[f"c3ami{p}"], writes=[f"c3qs{p}"], name=f"qs{t}")
        s.add("pool", (lambda q, p=p: q.tensor_tensor(
            c3["ag"][p], c3["am"][p], mwcol[:, 0:1], ALU.mult)),
            reads=[f"c3am{p}", "mwcol"], writes=[f"c3ag{p}"], name=f"ag{t}")
        s.add("pool", (lambda q, p=p: q.tensor_tensor(
            c3["au"][p], c3["am"][p], mwcol[:, 1:2], ALU.mult)),
            reads=[f"c3am{p}", "mwcol"], writes=[f"c3au{p}"], name=f"au{t}")
        s.add("dve", (lambda q, p=p: q.tensor_scalar(
            rq4, ln3x, c3["qs"][p], MAGIC, ALU.mult, ALU.add)),
            reads=["ln3x", f"c3qs{p}"], writes=["rq4"], name=f"rqo{t}")
        s.add("dve", lambda q: q.tensor_scalar(
            kq, rq4, MAGIC, None, ALU.subtract),
            reads=["rq4"], writes=["kq_r"], name=f"kqo{t}")

    def transp4a(t):
        p = t % 2
        for cj in range(8):
            s.add("pe", (lambda q, cj=cj: q.transpose(
                bank_bf(6)[:, cj * 128:(cj + 1) * 128],
                kq[:, cj * 128:(cj + 1) * 128], ident_bf)),
                reads=["kq_r", "ident_bf"], writes=[PB[6]], name=f"tkq{t}_{cj}")
        s.add("act", (lambda q, p=p: q.copy(
            kqT[:, p, :, :], bank_bf(6).rearrange("x (a b) -> x a b",
                                                  a=8, b=128))),
            reads=[PB[6]], writes=[f"kqT{p}"], name=f"ekq{t}")

    def mm4a(t):
        p = t % 2
        for blk in range(8):
            gb = (blk % 3) * 2
            ub = gb + 1
            for cj in range(8):
                s.add("pe", (lambda q, blk=blk, cj=cj, gb=gb, p=p: q.matmul(
                    bank(gb), kqT[:, p, cj, :],
                    WgT[:, cj, blk * 512:(blk + 1) * 512],
                    start=(cj == 0), stop=(cj == 7))),
                    reads=[f"kqT{p}", "WgT"], writes=[PB[gb]],
                    name=f"gg{t}_{blk}_{cj}")
            for cj in range(8):
                s.add("pe", (lambda q, blk=blk, cj=cj, ub=ub, p=p: q.matmul(
                    bank(ub), kqT[:, p, cj, :],
                    WuT[:, cj, blk * 512:(blk + 1) * 512],
                    start=(cj == 0), stop=(cj == 7))),
                    reads=[f"kqT{p}", "WuT"], writes=[PB[ub]],
                    name=f"guu{t}_{blk}_{cj}")
            sp_ = blk % 2
            s.add("act", (lambda q, gb=gb, sp_=sp_, p=p: q.activation(
                sg4[:, sp_, :], bank(gb), AF.Silu, scale=c3["ag"][p])),
                reads=[PB[gb], f"c3ag{p}", f"sg{sp_}"], writes=[f"sg{sp_}"],
                name=f"si{t}_{blk}")
            s.add("dve", (lambda q, blk=blk, ub=ub, sp_=sp_, p=p:
                          q.scalar_tensor_tensor(
                out=gu[:, blk * 512:(blk + 1) * 512], in0=bank(ub),
                scalar=c3["au"][p], in1=sg4[:, sp_, :],
                op0=ALU.mult, op1=ALU.mult)),
                reads=[PB[ub], f"c3au{p}", f"sg{sp_}"], writes=["gu_r"],
                name=f"gm{t}_{blk}")
            s.add("dve", (lambda q, blk=blk: q.tensor_reduce(
                gmax8[:, blk:blk + 1], gu[:, blk * 512:(blk + 1) * 512],
                AX.X, ALU.max, apply_absolute_value=True)),
                reads=["gu_r"], writes=["gmax8"], name=f"gx{t}_{blk}")

    def tail4a(t):
        p = t % 2
        s.add("dve", (lambda q, p=p: q.tensor_reduce(
            c3["am2"][p], gmax8[:, 0:8], AX.X, ALU.max)),
            reads=["gmax8"], writes=[f"c3am2{p}"], name=f"am2{t}")
        s.add("dve", (lambda q, p=p: q.tensor_scalar(
            c3["am2c"][p], c3["am2"][p], 1e-5, None, ALU.max)),
            reads=[f"c3am2{p}"], writes=[f"c3am2c{p}"], name=f"am2c{t}")
        s.add("dve", (lambda q, p=p: q.reciprocal(c3["t3"][p], c3["am2c"][p])),
              reads=[f"c3am2c{p}"], writes=[f"c3t3{p}"], name=f"am2i{t}")
        s.add("dve", (lambda q, p=p: q.tensor_scalar(
            c3["qs2"][p], c3["t3"][p], 127.0, None, ALU.mult)),
            reads=[f"c3t3{p}"], writes=[f"c3qs2{p}"], name=f"qs2{t}")
        s.add("pool", (lambda q, t=t, p=p: q.tensor_tensor(
            adal[:, t:t + 1], c3["am2c"][p], mwcol[:, 2:3], ALU.mult)),
            reads=[f"c3am2c{p}", "mwcol"], writes=["adal"], name=f"ad{t}")
        s.add("dve", (lambda q, p=p: q.tensor_scalar(
            gu, gu, c3["qs2"][p], MAGIC, ALU.mult, ALU.add)),
            reads=["gu_r", f"c3qs2{p}"], writes=["gu_r"], name=f"gq1{t}")
        s.add("dve", lambda q: q.tensor_scalar(
            gq, gu, MAGIC, None, ALU.subtract),
            reads=["gu_r"], writes=["gq_r"], name=f"gq2{t}")
        s.dma("sp", gq_scr[t * 128:(t + 1) * 128, :], gq,
              reads=["gq_r"], writes=[f"gqs.{t}"], name=f"stgq{t}")

    chain4a(0)
    transp4a(0)
    for t in range(NTO):
        if t + 1 < NTO:
            chain4a(t + 1)
        mm4a(t)
        tail4a(t)
        if t + 1 < NTO:
            transp4a(t + 1)

    s.barrier("bar_4b")
    for dc in range(8):
        p = dc % 2
        s.dma("sp", wstg4b[:, p, :], w_down[dc * 128:(dc + 1) * 128, :],
              writes=[f"wstb{p}"], name=f"twd{dc}")
        s.add("dve", (lambda q, p=p: q.tensor_scalar(
            wstg4b[:, p, :], wstg4b[:, p, :], swcol[:, 2:3], MAGIC,
            ALU.mult, ALU.add)),
            reads=[f"wstb{p}", "swcol"], writes=[f"wstb{p}"], name=f"td1{dc}")
        s.add("dve", (lambda q, p=p: q.tensor_scalar(
            wstg4b[:, p, :], wstg4b[:, p, :], MAGIC, None, ALU.subtract)),
            reads=[f"wstb{p}"], writes=[f"wstb{p}"], name=f"td2{dc}")
        s.add("dve", (lambda q, p=p: q.tensor_scalar(
            tern4b, wstg4b[:, p, :], 1.0, -1.0, ALU.min, ALU.max)),
            reads=[f"wstb{p}"], writes=["tern4b"], name=f"td3{dc}")
        for g4 in range(4):
            tb = 4 + (g4 % 2)
            for j in range(8):
                ic = g4 * 8 + j
                s.add("pe", (lambda q, ic=ic, j=j, tb=tb: q.transpose(
                    bank_bf(tb)[:, j * 128:(j + 1) * 128],
                    tern4b[:, ic * 128:(ic + 1) * 128], ident_bf)),
                    reads=["tern4b", "ident_bf"], writes=[PB[tb]],
                    name=f"td4{dc}_{ic}")
            s.add("dve", (lambda q, dc=dc, g4=g4, tb=tb: q.tensor_copy(
                WdT[:, g4 * 8:(g4 + 1) * 8, dc * 128:(dc + 1) * 128],
                bank_bf(tb).rearrange("p (a b) -> p a b", a=8, b=128))),
                reads=[PB[tb]], writes=["WdT"], name=f"td5{dc}_{g4}")

    def transp4b(t):
        p = t % 2
        for g4 in range(4):
            tb = 4 + (g4 % 2)
            for j in range(8):
                ic = g4 * 8 + j
                s.add("pe", (lambda q, ic=ic, j=j, p=p, tb=tb: q.transpose(
                    bank_bf(tb)[:, j * 128:(j + 1) * 128],
                    gq4[:, p, ic * 128:(ic + 1) * 128], ident_bf)),
                    reads=[f"gq4{p}", "ident_bf"], writes=[PB[tb]],
                    name=f"tgq{t}_{ic}")
            if g4 % 2 == 0:
                s.add("dve", (lambda q, g4=g4, tb=tb, p=p: q.tensor_copy(
                    gqT[:, p, g4 * 8:(g4 + 1) * 8, :].rearrange(
                        "x a b -> x (a b)"),
                    bank_bf(tb))),
                    reads=[PB[tb]], writes=[f"gqT{p}"], name=f"egq{t}_{g4}")
            else:
                s.add("act", (lambda q, g4=g4, tb=tb, p=p: q.copy(
                    gqT[:, p, g4 * 8:(g4 + 1) * 8, :].rearrange(
                        "x a b -> x (a b)"),
                    bank_bf(tb))),
                    reads=[PB[tb]], writes=[f"gqT{p}"], name=f"egq{t}_{g4}")

    def mm4b(t):
        p = t % 2
        for ds in range(2):
            for ic in range(32):
                s.add("pe", (lambda q, ds=ds, ic=ic, p=p: q.matmul(
                    bank(ds), gqT[:, p, ic, :],
                    WdT[:, ic, ds * 512:(ds + 1) * 512],
                    start=(ic == 0), stop=(ic == 31))),
                    reads=[f"gqT{p}", "WdT"], writes=[PB[ds]],
                    name=f"dn{t}_{ds}_{ic}")
            s.add("dve", (lambda q, ds=ds, t=t: q.tensor_scalar(
                wd4[:, ds * 512:(ds + 1) * 512], bank(ds),
                adal[:, t:t + 1], None, ALU.mult)),
                reads=[PB[ds], "adal"], writes=["wd4_r"], name=f"ed{t}_{ds}")

    def tail4b(t):
        p = t % 2
        s.dma("sp", xt4b[:, p, :], xnew_scr[t * 128:(t + 1) * 128, :],
              reads=[f"xn.{t}"], writes=[f"xtb{p}"], name=f"ldxb{t}")
        s.add("dve", (lambda q, p=p: q.tensor_reduce(
            c3["s"][p], wd4, AX.X, ALU.add)),
            reads=["wd4_r"], writes=[f"c3s{p}"], name=f"l4s{t}")
        s.add("act", (lambda q, p=p: q.activation(
            dummy1k, wd4, AF.Square, accum_out=c3["q"][p])),
            reads=["wd4_r"], writes=["dummy1k"], accum=[f"c3q{p}"],
            name=f"l4q{t}")
        s.add("pool", (lambda q, p=p: q.tensor_scalar(
            c3["m"][p], c3["s"][p], 1.0 / C, None, ALU.mult)),
            reads=[f"c3s{p}"], writes=[f"c3m{p}"], name=f"l4m{t}")
        s.add("pool", (lambda q, p=p: q.tensor_tensor(
            c3["t1"][p], c3["m"][p], c3["m"][p], ALU.mult)),
            reads=[f"c3m{p}"], writes=[f"c3t1{p}"], name=f"l4mm{t}")
        s.add("pool", (lambda q, p=p: q.tensor_scalar(
            c3["t2"][p], c3["q"][p], 1.0 / C, EPS, ALU.mult, ALU.add)),
            reads=[f"c3q{p}"], writes=[f"c3t2{p}"], name=f"l4qq{t}")
        s.add("pool", (lambda q, p=p: q.tensor_tensor(
            c3["t2"][p], c3["t2"][p], c3["t1"][p], ALU.subtract)),
            reads=[f"c3t2{p}", f"c3t1{p}"], writes=[f"c3t2{p}"],
            name=f"l4v{t}")
        s.add("pool", (lambda q, p=p: q.tensor_tensor(
            c3["r"][p], c3["t2"][p], nh_col, ALU.pow)),
            reads=[f"c3t2{p}", "neghalf"], writes=[f"c3r{p}"], name=f"l4r{t}")
        s.add("pool", (lambda q, p=p: q.tensor_tensor(
            c3["t1"][p], c3["m"][p], c3["r"][p], ALU.mult)),
            reads=[f"c3m{p}", f"c3r{p}"], writes=[f"c3t1{p}"], name=f"l4mr{t}")
        s.add("dve", (lambda q, p=p: q.scalar_tensor_tensor(
            out=xo4, in0=wd4, scalar=c3["r"][p], in1=xt4b[:, p, :],
            op0=ALU.mult, op1=ALU.add)),
            reads=["wd4_r", f"c3r{p}", f"xtb{p}"], writes=["xo4_r"],
            name=f"rs4{t}")
        s.add("dve", (lambda q, p=p: q.tensor_scalar(
            xo4, xo4, c3["t1"][p], None, ALU.subtract)),
            reads=["xo4_r", f"c3t1{p}"], writes=["xo4_r"], name=f"rs5{t}")
        s.dma("sp", y_out[t * 128:(t + 1) * 128, :], xo4,
              reads=["xo4_r"], writes=[f"y.{t}"], name=f"sty{t}")

    s.dma("sp", gq4[:, 0, :], gq_scr[0:128, :],
          reads=["gqs.0"], writes=["gq40"], name="ldgq0")
    transp4b(0)
    for t in range(NTO):
        if t + 1 < NTO:
            s.dma("sp", gq4[:, (t + 1) % 2, :],
                  gq_scr[(t + 1) * 128:(t + 2) * 128, :],
                  reads=[f"gqs.{t + 1}"], writes=[f"gq4{(t + 1) % 2}"],
                  name=f"ldgq{t + 1}")
        mm4b(t)
        tail4b(t)
        if t + 1 < NTO:
            transp4b(t + 1)

    if taps:
        for fn in taps.values():
            fn(nc, s, locals())

    s.add("sp", lambda q: None, reads=[f"y.{t}" for t in range(NTO)],
          name="sink")
    s.build()
    return nc


_CACHE = {}


def kernel(L0_attn_x, L0_attn_W_attn, L0_attn_W_proj, L0_attn_lora_attn_A,
           L0_attn_lora_attn_B, L0_attn_lora_proj_A, L0_attn_lora_proj_B,
           L0_attn_bn_gamma, L0_attn_bn_beta, L0_ffn_w_gate, L0_ffn_w_up,
           L0_ffn_w_down):
    if "nc" not in _CACHE:
        _CACHE["nc"] = _build()
    nc = _CACHE["nc"]
    x = np.ascontiguousarray(np.asarray(L0_attn_x, dtype=np.float32))
    B = x.shape[0]
    common = {
        "W_attn": np.ascontiguousarray(L0_attn_W_attn, dtype=np.float32),
        "W_proj": np.ascontiguousarray(L0_attn_W_proj, dtype=np.float32),
        "lA": np.ascontiguousarray(L0_attn_lora_attn_A, dtype=np.float32),
        "lB": np.ascontiguousarray(L0_attn_lora_attn_B, dtype=np.float32),
        "pA": np.ascontiguousarray(L0_attn_lora_proj_A, dtype=np.float32),
        "pB": np.ascontiguousarray(L0_attn_lora_proj_B, dtype=np.float32),
        "gamma": np.ascontiguousarray(L0_attn_bn_gamma, dtype=np.float32),
        "beta": np.ascontiguousarray(L0_attn_bn_beta, dtype=np.float32),
        "w_gate": np.ascontiguousarray(L0_ffn_w_gate, dtype=np.float32),
        "w_up": np.ascontiguousarray(L0_ffn_w_up, dtype=np.float32),
        "w_down": np.ascontiguousarray(L0_ffn_w_down, dtype=np.float32),
    }
    in_maps = []
    for b in range(B):
        m = dict(common)
        m["x_all"] = np.ascontiguousarray(
            np.roll(x, -b, axis=0).reshape(B * T, C))
        in_maps.append(m)
    res = run_bass_kernel_spmd(nc, in_maps, core_ids=list(range(B)))
    out = np.stack([res.results[b]["y"] for b in range(B)], axis=0)
    return out.astype(np.float32)


# version 13
# speedup vs baseline: 1.0396x; 1.0396x over previous
import numpy as np
import concourse.bass as bass
import concourse.mybir as mybir
from concourse.bass_utils import run_bass_kernel_spmd

F32 = mybir.dt.float32
BF16 = mybir.dt.bfloat16
AF = mybir.ActivationFunctionType
ALU = mybir.AluOpType
AX = mybir.AxisListType

MAGIC = 12582912.0
EPS = 1e-5
T = 4096
C = 1024
NT = 256
NTO = 32

ENGINES = ("pe", "act", "dve", "pool", "sp")


class _Op:
    __slots__ = ("idx", "eng", "emit", "reads", "writes", "name", "is_async",
                 "accum", "deps", "needs_inc", "event")

    def __init__(self, idx, eng, emit, reads, writes, name, is_async, accum):
        self.idx = idx
        self.eng = eng
        self.emit = emit
        self.reads = tuple(reads)
        self.writes = tuple(writes)
        self.name = name
        self.is_async = is_async
        self.accum = tuple(accum)
        self.deps = []
        self.needs_inc = False
        self.event = None


class _Sched:

    def __init__(self, nc):
        self.nc = nc
        self.ops = []
        self._has_bar = False

    def add(self, eng, emit, reads=(), writes=(), name="", is_async=False,
            accum=()):
        assert eng in ENGINES, eng
        reads = tuple(reads) + ("__bar",)
        op = _Op(len(self.ops), eng, emit, reads,
                 tuple(writes) + tuple(accum), name, is_async, accum)
        self.ops.append(op)
        return op

    def dma(self, eng, out, in_, reads=(), writes=(), name="dma"):
        def emit(q, out=out, in_=in_):
            return q.dma_start(out=out, in_=in_)
        return self.add(eng, emit, reads=reads, writes=writes, name=name,
                        is_async=True)

    def barrier(self, name="bar"):
        self._has_bar = True
        op = _Op(len(self.ops), "pool", lambda q: q.engine_nop(),
                 (), ("__bar",), name, False, ())
        self.ops.append(op)
        return op

    def _analyze(self):
        R = 16
        dma_cnt = {e: 0 for e in ENGINES}
        self._dma_gate = {}
        for op in self.ops:
            if op.is_async:
                n = dma_cnt[op.eng]
                dma_cnt[op.eng] += 1
                r = n % R
                use = n // R + 1
                key = f"dma_{op.eng}_{r}"
                op.event = (key, 16 * use)
                if use > 1:
                    self._dma_gate[op.idx] = (key, 16 * (use - 1))
        self._dma_cnt = dma_cnt
        self._R = R
        last_write = {}
        readers = {}
        accum_res = {}
        for op in self.ops:
            deps = set()
            for r in op.reads:
                j = last_write.get(r)
                if j is not None:
                    deps.add(j)
                    if accum_res.get(r) == j and self.ops[j].eng == op.eng:
                        raise RuntimeError(
                            f"{op.name} reads accum resource {r} on same "
                            f"engine as producer {self.ops[j].name}")
            for w in op.writes:
                j = last_write.get(w)
                if j is not None:
                    deps.add(j)
                deps.update(readers.get(w, ()))
            best = {}
            for j in deps:
                pj = self.ops[j]
                key = pj.event[0] if pj.is_async else ("eng", pj.eng)
                if key not in best or j > best[key]:
                    best[key] = j
            final = []
            for key, j in best.items():
                pj = self.ops[j]
                if (pj.eng == op.eng == "pe" and not pj.is_async
                        and not op.is_async):
                    continue
                final.append(j)
            op.deps = sorted(final)
            for w in op.writes:
                last_write[w] = op.idx
                readers[w] = []
                if w in op.accum:
                    accum_res[w] = op.idx
                else:
                    accum_res.pop(w, None)
            for r in op.reads:
                readers.setdefault(r, []).append(op.idx)
        for op in self.ops:
            for j in op.deps:
                if not self.ops[j].is_async:
                    self.ops[j].needs_inc = True
        eng_cnt = {e: 0 for e in ENGINES}
        for op in self.ops:
            if op.is_async:
                pass
            elif op.needs_inc:
                eng_cnt[op.eng] += 1
                op.event = ("eng_" + op.eng, eng_cnt[op.eng])
        self._eng_cnt = eng_cnt

    def build(self):
        nc = self.nc
        self._analyze()
        per_eng = {e: [o for o in self.ops if o.eng == e] for e in ENGINES}
        import contextlib
        with contextlib.ExitStack() as stack:
            sems = {}
            for e in ENGINES:
                if self._eng_cnt[e] > 0:
                    sems["eng_" + e] = stack.enter_context(
                        nc.semaphore("sem_eng_" + e))
                if self._dma_cnt[e] > 0:
                    for r in range(self._R):
                        k = f"dma_{e}_{r}"
                        sems[k] = stack.enter_context(nc.semaphore("sem_" + k))
            block = stack.enter_context(nc.Block())

            def emit_engine(q, e):
                waited = {}
                for op in per_eng[e]:
                    gate = self._dma_gate.get(op.idx)
                    if gate is not None:
                        sk, val = gate
                        if waited.get(sk, 0) < val:
                            q.wait_ge(sems[sk], val)
                            waited[sk] = val
                    for j in op.deps:
                        sk, val = self.ops[j].event
                        if waited.get(sk, 0) >= val:
                            continue
                        q.wait_ge(sems[sk], val)
                        waited[sk] = val
                    ins = op.emit(q)
                    if op.is_async:
                        ins.then_inc(sems[op.event[0]], 16)
                    elif op.needs_inc:
                        assert ins is not None, op.name
                        ins.then_inc(sems["eng_" + e], 1)

            if per_eng["sp"]:
                block.sync(lambda q: emit_engine(q, "sp"))
            if per_eng["pe"]:
                block.tensor(lambda q: emit_engine(q, "pe"))
            if per_eng["act"]:
                block.scalar(lambda q: emit_engine(q, "act"))
            if per_eng["dve"]:
                block.vector(lambda q: emit_engine(q, "dve"))
            if per_eng["pool"]:
                block.gpsimd(lambda q: emit_engine(q, "pool"))


def _build(taps=None, dbg=False):
    nc = bass.Bass("TRN2", target_bir_lowering=False)

    x_all = nc.dram_tensor("x_all", [NT * 128, C], F32, kind="ExternalInput").ap()
    W_attn = nc.dram_tensor("W_attn", [3 * C, C], F32, kind="ExternalInput").ap()
    W_proj = nc.dram_tensor("W_proj", [C, C], F32, kind="ExternalInput").ap()
    lA = nc.dram_tensor("lA", [C, 64], F32, kind="ExternalInput").ap()
    lB = nc.dram_tensor("lB", [64, 3 * C], F32, kind="ExternalInput").ap()
    pA = nc.dram_tensor("pA", [C, 64], F32, kind="ExternalInput").ap()
    pB = nc.dram_tensor("pB", [64, C], F32, kind="ExternalInput").ap()
    gamma = nc.dram_tensor("gamma", [3 * C], F32, kind="ExternalInput").ap()
    beta = nc.dram_tensor("beta", [3 * C], F32, kind="ExternalInput").ap()
    w_gate = nc.dram_tensor("w_gate", [4 * C, C], F32, kind="ExternalInput").ap()
    w_up = nc.dram_tensor("w_up", [4 * C, C], F32, kind="ExternalInput").ap()
    w_down = nc.dram_tensor("w_down", [C, 4 * C], F32, kind="ExternalInput").ap()
    y_out = nc.dram_tensor("y", [T, C], F32, kind="ExternalOutput").ap()

    _scr_kind = "ExternalOutput" if dbg else "Internal"
    lnx_scr = nc.dram_tensor("lnx_scr", [NT * 128, C], BF16, kind=_scr_kind).ap()
    xa_scr = nc.dram_tensor("xa_scr", [3 * C, T], BF16, kind=_scr_kind).ap()
    loraL = nc.dram_tensor("loraL", [C, 3 * C], F32, kind=_scr_kind).ap()
    loraL_r = loraL.rearrange("a (b c) -> (a b) c", b=3, c=C)
    xnew_scr = nc.dram_tensor("xnew_scr", [T, C], F32, kind=_scr_kind).ap()
    gq_scr = nc.dram_tensor("gq_scr", [T, 4 * C], BF16, kind=_scr_kind).ap()
    if dbg:
        dbg_wq = nc.dram_tensor("dbg_wq", [128, 8, 3072], F32, kind="ExternalOutput").ap()
        dbg_wm = nc.dram_tensor("dbg_wm", [128, 24, 1024], F32, kind="ExternalOutput").ap()
        dbg_lnfm = nc.dram_tensor("dbg_lnfm", [128, 8, 512], F32, kind="ExternalOutput").ap()
        dbg_ps = nc.dram_tensor("dbg_ps", [128, 512], F32, kind="ExternalOutput").ap()
        dbg_wq2 = nc.dram_tensor("dbg_wq2", [128, 8, 128], F32, kind="ExternalOutput").ap()
        dbg_wqB = nc.dram_tensor("dbg_wqB", [128, 8, 128], F32, kind="ExternalOutput").ap()
        dbg_S = nc.dram_tensor("dbg_S", [128, 24], F32, kind="ExternalOutput").ap()
        dbg_Q = nc.dram_tensor("dbg_Q", [128, 24], F32, kind="ExternalOutput").ap()
        dbg_uT = nc.dram_tensor("dbg_uT", [128, 8], F32, kind="ExternalOutput").ap()
        dbg_wqC = nc.dram_tensor("dbg_wqC", [128, 8, 128], F32, kind="ExternalOutput").ap()
        dbg_wqD = nc.dram_tensor("dbg_wqD", [128, 8, 128], F32, kind="ExternalOutput").ap()

        def dump_wq(dst, tag):
            for cj_ in range(8):
                s.add("dve", (lambda q, cj_=cj_: q.tensor_copy(
                    dummy1k[:, 0:128], WqkvT[:, cj_, 0:128])),
                    reads=["WqkvT"], writes=["dummy1k"], name="dq" + tag)
                s.dma("sp", dst[:, cj_, :], dummy1k[:, 0:128],
                      reads=["dummy1k"], writes=["dbgdummy"], name="dqd" + tag)
        dbg_lf2 = nc.dram_tensor("dbg_lf2", [128, 8, 512], F32, kind="ExternalOutput").ap()
        dbg_stg = nc.dram_tensor("dbg_stg", [128, 4, 512], BF16, kind="ExternalOutput").ap()
        dbg_abn = nc.dram_tensor("dbg_abn", [128, 24], F32, kind="ExternalOutput").ap()
        dbg_bbn = nc.dram_tensor("dbg_bbn", [128, 24], F32, kind="ExternalOutput").ap()
        dbg_G = nc.dram_tensor("dbg_G", [C, C], F32, kind="ExternalOutput").ap()
        dbg_u = nc.dram_tensor("dbg_u", [1, C], F32, kind="ExternalOutput").ap()

    SB_BASE = (nc.sbuf_base + 31) & ~31
    SB_TOP = nc.sbuf_top
    _names = set()

    def sb(name, shape, dtype, off):
        assert name not in _names, name
        _names.add(name)
        n = int(np.prod(shape[1:])) * mybir.dt.size(dtype)
        assert off % 32 == 0, (name, off)
        assert SB_BASE + off + n <= SB_TOP, (name, off, n)
        return nc.alloc_sbuf_tensor_at(name, list(shape), dtype,
                                       offset=SB_BASE + off).ap()

    ident_bf = sb("ident_bf", [128, 128], BF16, 0)
    ident_f32 = sb("ident_f32", [128, 128], F32, 256)
    maskc = sb("maskc", [128, 128], F32, 768)
    ones_bf = sb("ones_bf", [128, 128], BF16, 1280)
    ones_f32 = sb("ones_f32", [128, 128], F32, 1536)
    neghalf = sb("neghalf", [128, 32], F32, 2048)
    abn = sb("abn", [128, 24], F32, 2176)
    bbn = sb("bbn", [128, 24], F32, 2304)
    smalls = sb("smalls", [128, 240], F32, 2432)
    _scol = [0]

    def col(n=1):
        c0 = _scol[0]
        _scol[0] += n
        assert _scol[0] <= 240
        return smalls[:, c0:c0 + n]

    dummy1k = sb("dummy1k", [128, 1024], F32, 3392)
    negmag = sb("negmag", [128, 1], F32, 7808)
    adal = sb("adal", [128, 32], F32, 7488)
    mwcol = sb("mwcol", [128, 4], F32, 7616)
    swcol = sb("swcol", [128, 4], F32, 7776)

    Wm_tm = sb("Wm_tm", [128, 24, 1024], BF16, 8192)
    WqkvT = sb("WqkvT", [128, 8, 3072], BF16, 57344)
    G_sb = sb("G_sb", [128, 8, 1024], F32, 106496)
    u_sb = sb("u_sb", [1, 1024], F32, 139264)
    gammaF = sb("gammaF", [128, 24], F32, 143360)
    betaF = sb("betaF", [128, 24], F32, 143488)
    uT_f = sb("uT_f", [128, 8], F32, 143616)
    uT_bf = sb("uT_bf", [128, 8], BF16, 143648)
    Scol = sb("Scol", [128, 24], F32, 143680)
    Qcol = sb("Qcol", [128, 24], F32, 143808)
    meanF = sb("meanF", [128, 24], F32, 143936)
    vepsF = sb("vepsF", [128, 24], F32, 144064)
    d24 = sb("d24", [128, 24], F32, 144192)
    TBASE = 145408
    wstg = sb("wstg", [128, 2, 1024], F32, TBASE)
    lstg = sb("lstg", [128, 2, 1024], F32, TBASE + 8192)
    AT_sb = sb("AT_sb", [64, 1024], F32, TBASE + 16384)
    B_sb = sb("B_sb", [64, 3072], F32, TBASE + 20480)
    Lstg = sb("Lstg", [128, 3072], F32, TBASE + 32768)
    wsum = sb("wsum", [128, 1024], F32, TBASE + 45056)
    xin = sb("xin", [128, 8, 1024], F32, TBASE)
    lnx = sb("lnx", [128, 8, 1024], BF16, TBASE + 32768)
    lnb = sb("lnb", [128, 8, 1024], BF16, TBASE + 49152)
    G_bf = sb("G_bf", [128, 8, 1024], BF16, TBASE + 49152)
    ln1b = sb("ln1b", [128, 4, 1024], BF16, TBASE)
    lnfm = sb("lnfm", [128, 2, 8, 512], BF16, TBASE + 8192)
    stg = sb("stg", [128, 2, 4, 512], BF16, TBASE + 24576)

    Kfm = sb("Kfm", [128, 8, 4096], BF16, 8192)
    Vtm = sb("Vtm", [128, 32, 1024], BF16, 73728)
    WpT = sb("WpT", [128, 8, 1024], BF16, 139264)
    WmpT = sb("WmpT", [128, 8, 1024], BF16, 155648)
    S_f = sb("S_f", [128, 4096], F32, 172032)
    S_bf = S_f.bitcast(BF16)
    wstg2 = sb("wstg2", [128, 2, 1024], F32, 172032)
    pAT = sb("pAT", [64, 1024], F32, 180224)
    pATb = sb("pATb", [64, 1024], BF16, 184320)
    pBb = sb("pBb", [64, 1024], BF16, 186368)
    qpan = sb("qpan", [128, 8, 128], BF16, 188416)
    Pscr2 = 188416 + 2048
    vbn = sb("vbn", [128, 8, 1024], BF16, 188416)
    XT = sb("XT", [128, 1024], F32, 196608)
    qfm = sb("qfm", [128, 8, 128], BF16, 200704)
    yfm = sb("yfm", [128, 8, 128], BF16, 202752)
    zfm = sb("zfm", [128, 8, 128], BF16, 204800)
    wsb = sb("wsb", [128, 8, 128], F32, 206848)
    wtm = sb("wtm", [128, 1024], F32, 206848)
    gmaxs = sb("gmaxs", [128, 12], F32, 210944)
    lsums = sb("lsums", [128, 12], F32, 210944 + 64)
    Pscr = sb("Pscr", [128, 2, 512], BF16, 188416 + 2048)
    ytm = sb("ytm", [128, 1024], BF16, 192512)
    ztm = sb("ztm", [128, 1024], BF16, 194560)

    WgT = sb("WgT", [128, 8, 4096], BF16, 8192)
    WuT = sb("WuT", [128, 8, 4096], BF16, 73728)
    wstg4 = sb("wstg4", [128, 2, 1024], F32, 139264)
    xt4 = sb("xt4", [128, 2, 1024], F32, 147456)
    ln3x = sb("ln3x", [128, 1024], F32, 155648)
    rq4 = sb("rq4", [128, 1024], F32, 159744)
    kq = sb("kq", [128, 1024], BF16, 163840)
    kqT = sb("kqT", [128, 2, 8, 128], BF16, 165888)
    sg4 = sb("sg4", [128, 2, 512], F32, 169984)
    gu = sb("gu", [128, 4096], F32, 174080)
    gq = sb("gq", [128, 4096], BF16, 190464)
    absc = sb("absc", [128, 32], F32, 198656)
    gmax8 = sb("gmax8", [128, 8], F32, 198784)
    ms1 = sb("ms1", [1, 4], F32, 198816)

    WdT = sb("WdT", [128, 32, 1024], BF16, 8192)
    wstg4b = sb("wstg4b", [128, 2, 4096], F32, 73728)
    tern4b = sb("tern4b", [128, 4096], BF16, 106496)
    gq4 = sb("gq4", [128, 2, 4096], BF16, 114688)
    gqT = sb("gqT", [128, 2, 32, 128], BF16, 131072)
    wd4 = sb("wd4", [128, 1024], F32, 147456)
    xt4b = sb("xt4b", [128, 2, 1024], F32, 151552)
    xo4 = sb("xo4", [128, 1024], F32, 159744)

    pall = nc.alloc_psum_tensor("pall", [128, 4096], F32).ap()
    pall_bf = pall.bitcast(BF16)

    def bank(i):
        return pall[:, i * 512:(i + 1) * 512]

    def bank_bf(i):
        return pall_bf[:, i * 1024:(i + 1) * 1024]

    PB = [f"pb{i}" for i in range(8)]

    s = _Sched(nc)

    s.add("pool", lambda q: q.memset(ident_bf, 0.0), writes=["ident_bf"])
    s.add("pool", lambda q: q.affine_select(
        out=ident_bf, in_=ident_bf, compare_op=ALU.not_equal, fill=1.0, base=0,
        pattern=[[-1, 128]], channel_multiplier=1),
        reads=["ident_bf"], writes=["ident_bf"], name="id_bf")
    s.add("pool", lambda q: q.memset(ident_f32, 0.0), writes=["ident_f32"])
    s.add("pool", lambda q: q.affine_select(
        out=ident_f32, in_=ident_f32, compare_op=ALU.not_equal, fill=1.0,
        base=0, pattern=[[-1, 128]], channel_multiplier=1),
        reads=["ident_f32"], writes=["ident_f32"], name="id_f32")
    s.add("pool", lambda q: q.memset(maskc, 0.0), writes=["maskc"])
    s.add("pool", lambda q: q.affine_select(
        out=maskc, in_=maskc, compare_op=ALU.is_ge, fill=-1e30, base=0,
        pattern=[[-1, 128]], channel_multiplier=1),
        reads=["maskc"], writes=["maskc"], name="mk_mask")
    s.add("pool", lambda q: q.memset(ones_bf, 1.0), writes=["ones_bf"])
    s.add("pool", lambda q: q.memset(ones_f32, 1.0), writes=["ones_f32"])
    s.add("pool", lambda q: q.memset(neghalf, -0.5), writes=["neghalf"])
    s.add("pool", lambda q: q.memset(negmag, -MAGIC), writes=["negmag"])

    for i in range(8):
        p = i % 2
        s.dma("sp", lstg[:, p, 0:64], lA[i * 128:(i + 1) * 128, :],
              writes=[f"lstg{p}"], name=f"ldA{i}")
        s.add("pe", (lambda q, i=i, p=p: q.transpose(
            bank(6)[0:64, (i % 4) * 128:(i % 4) * 128 + 128],
            lstg[:, p, 0:64], ident_f32)),
            reads=[f"lstg{p}", "ident_f32"], writes=[PB[6]], name=f"trA{i}")
        s.add("dve", (lambda q, i=i: q.tensor_copy(
            AT_sb[:, i * 128:(i + 1) * 128],
            bank(6)[0:64, (i % 4) * 128:(i % 4) * 128 + 128])),
            reads=[PB[6]], writes=["AT_sb"], name=f"evA{i}")
    s.dma("sp", B_sb, lB, writes=["B_sb"], name="ldB")
    for r in range(8):
        for n in range(6):
            b = n % 2
            s.add("pe", (lambda q, r=r, n=n, b=b: q.matmul(
                bank(b), AT_sb[:, r * 128:(r + 1) * 128],
                B_sb[:, n * 512:(n + 1) * 512], start=True, stop=True)),
                reads=["AT_sb", "B_sb"], writes=[PB[b]], name=f"mmL{r}_{n}")
            s.add("dve", (lambda q, n=n, b=b: q.tensor_copy(
                Lstg[:, n * 512:(n + 1) * 512], bank(b))),
                reads=[PB[b]], writes=["Lstg"], name=f"evL{r}_{n}")
        s.dma("sp", loraL[r * 128:(r + 1) * 128, :], Lstg,
              reads=["Lstg"], writes=[f"loraL.{r}"], name=f"stL{r}")

    for dj in range(24):
        p = dj % 2
        s.dma("sp", wstg[:, p, :], W_attn[dj * 128:(dj + 1) * 128, :],
              writes=[f"wstg{p}"], name=f"ldWa{dj}")
        s.dma("sp", lstg[:, p, :], loraL_r[dj * 128:(dj + 1) * 128, :],
              reads=[f"loraL.{dj // 3}"], writes=[f"lstg{p}"],
              name=f"ldLr{dj}")
        s.add("dve", (lambda q, p=p, dj=dj: q.tensor_tensor(
            Wm_tm[:, dj, :], wstg[:, p, :], lstg[:, p, :], ALU.add)),
            reads=[f"wstg{p}", f"lstg{p}"], writes=["Wm_tm"], name=f"addW{dj}")
        tb = 2 + (dj % 2)
        for cj in range(8):
            s.add("pe", (lambda q, dj=dj, cj=cj, tb=tb: q.transpose(
                bank_bf(tb)[:, cj * 128:(cj + 1) * 128],
                Wm_tm[:, dj, cj * 128:(cj + 1) * 128], ident_bf)),
                reads=["Wm_tm", "ident_bf"], writes=[PB[tb]],
                name=f"trW{dj}_{cj}")
        s.add("dve", (lambda q, dj=dj, tb=tb: q.tensor_copy(
            WqkvT[:, :, dj * 128:(dj + 1) * 128],
            bank_bf(tb).rearrange("p (a b) -> p a b", a=8, b=128))),
            reads=[PB[tb]], writes=["WqkvT"], name=f"evW{dj}")

    if dbg:
        for a_ in range(8):
            for seg in range(3):
                s.add("dve", (lambda q, a_=a_, seg=seg: q.tensor_copy(
                    dummy1k, WqkvT[:, a_, seg * 1024:(seg + 1) * 1024])),
                    reads=["WqkvT"], writes=["dummy1k"], name="dbgcwq")
                s.dma("sp", dbg_wq[:, a_, seg * 1024:(seg + 1) * 1024], dummy1k,
                      reads=["dummy1k"], writes=["dbgdummy"], name="dbgwq")
        for a_ in range(24):
            s.add("dve", (lambda q, a_=a_: q.tensor_copy(
                dummy1k, Wm_tm[:, a_, :])),
                reads=["Wm_tm"], writes=["dummy1k"], name="dbgcwm")
            s.dma("sp", dbg_wm[:, a_, :], dummy1k,
                  reads=["dummy1k"], writes=["dbgdummy"], name="dbgwm")

    for nm, src, dst, c0 in (("g", gamma, gammaF, 0), ("b", beta, betaF, 32)):
        s.dma("sp", lstg[0:24, 0, 0:128], src.rearrange("(a p) -> a p", p=128),
              writes=["lstg0"], name=f"ldgb{nm}")
        s.add("pe", (lambda q, c0=c0: q.transpose(
            bank(4)[:, c0:c0 + 24], lstg[0:24, 0, 0:128],
            ident_f32[0:24, 0:24])),
            reads=["lstg0", "ident_f32"], writes=[PB[4]], name=f"trgb{nm}")
        s.add("dve", (lambda q, dst=dst, c0=c0: q.tensor_copy(
            dst, bank(4)[:, c0:c0 + 24])),
            reads=[PB[4]], writes=["gbF"], name=f"evgb{nm}")

    s.barrier("bar_1a")
    nh_col = neghalf[:, 0:1]
    st_s = [col() for _ in range(8)]
    st_q = [col() for _ in range(8)]
    st_m = [col() for _ in range(8)]
    st_r = [col() for _ in range(8)]
    st_t1 = [col() for _ in range(8)]
    st_t2 = [col() for _ in range(8)]

    def emit_ln(p, src_ap, src_res, dst_ap, dst_res):
        s.add("dve", (lambda q: q.tensor_reduce(st_s[p], src_ap, AX.X, ALU.add)),
              reads=[src_res], writes=[f"ls{p}"], name="red")
        s.add("act", (lambda q: q.activation(
            dummy1k, src_ap, AF.Square, accum_out=st_q[p])),
            reads=[src_res], writes=["dummy1k"], accum=[f"lq{p}"], name="sq")
        s.add("pool", (lambda q: q.tensor_scalar(
            st_m[p], st_s[p], 1.0 / C, None, ALU.mult)),
            reads=[f"ls{p}"], writes=[f"lm{p}"], name="m")
        s.add("pool", (lambda q: q.tensor_tensor(
            st_t1[p], st_m[p], st_m[p], ALU.mult)),
            reads=[f"lm{p}"], writes=[f"lt1{p}"], name="m2")
        s.add("pool", (lambda q: q.tensor_scalar(
            st_t2[p], st_q[p], 1.0 / C, EPS, ALU.mult, ALU.add)),
            reads=[f"lq{p}"], writes=[f"lt2{p}"], name="q2")
        s.add("pool", (lambda q: q.tensor_tensor(
            st_t2[p], st_t2[p], st_t1[p], ALU.subtract)),
            reads=[f"lt2{p}", f"lt1{p}"], writes=[f"lt2{p}"], name="ve")
        s.add("pool", (lambda q: q.tensor_tensor(
            st_r[p], st_t2[p], nh_col, ALU.pow)),
            reads=[f"lt2{p}", "neghalf"], writes=[f"lr{p}"], name="pw")
        s.add("dve", (lambda q: q.tensor_scalar(
            dst_ap, src_ap, st_m[p], st_r[p], ALU.subtract, ALU.mult)),
            reads=[src_res, f"lm{p}", f"lr{p}"], writes=[dst_res], name="nrm")

    def emit_ldx(t):
        p = t % 8
        s.dma("sp", xin[:, p, :], x_all[t * 128:(t + 1) * 128, :],
              writes=[f"R_xin{p}"], name=f"ldx{t}")

    for t in range(8):
        emit_ldx(t)
    for t in range(NT):
        p = t % 8
        emit_ln(p, xin[:, p, :], f"R_xin{p}", lnx[:, p, :], f"lnx{p}")
        for i in range(4):
            for h in range(2):
                bi = 2 * i + h
                s.add("pe", (lambda q, p=p, i=i, h=h, bi=bi, t=t: q.matmul(
                    bank(bi), lnx[:, p, i * 128:(i + 1) * 128],
                    lnx[:, p, h * 512:(h + 1) * 512],
                    start=(t == 0), stop=(t == NT - 1))),
                    reads=[f"lnx{p}"], writes=[PB[bi]], name=f"gA{t}_{bi}")
        s.dma("pool", lnx_scr[t * 128:(t + 1) * 128, :], lnx[:, p, :],
              reads=[f"lnx{p}"], writes=[f"lnxs.{t}"], name=f"spl{t}")
        if t + 8 < NT:
            emit_ldx(t + 8)

    if dbg:
        dump_wq(dbg_wqB, "B")
    for i in range(4):
        for h in range(2):
            bi = 2 * i + h
            s.add("dve", (lambda q, i=i, h=h, bi=bi: q.tensor_copy(
                G_sb[:, i, h * 512:(h + 1) * 512], bank(bi))),
                reads=[PB[bi]], writes=["G_sb"], name=f"evGA{bi}")

    for t in range(NT):
        p = t % 8
        s.dma("sp", lnb[:, p, :], lnx_scr[t * 128:(t + 1) * 128, :],
              reads=[f"lnxs.{t}"], writes=[f"lnb{p}"], name=f"ldlb{t}")
        for i in range(4):
            s.add("pe", (lambda q, p=p, i=i, t=t: q.matmul(
                bank(i), lnb[:, p, (4 + i) * 128:(5 + i) * 128],
                lnb[:, p, 512:1024],
                start=(t == 0), stop=(t == NT - 1))),
                reads=[f"lnb{p}"], writes=[PB[i]], name=f"gB{t}_{i}")
        for h in range(2):
            s.add("pe", (lambda q, p=p, h=h, t=t: q.matmul(
                bank(4 + h)[0:1, :], ones_bf[:, 0:1],
                lnb[:, p, h * 512:(h + 1) * 512],
                start=(t == 0), stop=(t == NT - 1))),
                reads=[f"lnb{p}", "ones_bf"], writes=[PB[4 + h]],
                name=f"gU{t}_{h}")

    if dbg:
        dump_wq(dbg_wqC, "C")
    for i in range(4):
        s.add("dve", (lambda q, i=i: q.tensor_copy(
            G_sb[:, 4 + i, 512:1024], bank(i))),
            reads=[PB[i]], writes=["G_sb"], name=f"evGB{i}")
    for h in range(2):
        s.add("dve", (lambda q, h=h: q.tensor_copy(
            u_sb[:, h * 512:(h + 1) * 512], bank(4 + h)[0:1, :])),
            reads=[PB[4 + h]], writes=["u_sb"], name=f"evU{h}")

    for k in range(4):
        tb = 6 + (k % 2)
        for cj in range(4):
            s.add("pe", (lambda q, k=k, cj=cj, tb=tb: q.transpose(
                bank(tb)[:, cj * 128:(cj + 1) * 128],
                G_sb[:, cj, 512 + k * 128:512 + (k + 1) * 128], ident_f32)),
                reads=["G_sb", "ident_f32"], writes=[PB[tb]],
                name=f"trG{k}_{cj}")
        s.add("dve", (lambda q, k=k, tb=tb: q.tensor_copy(
            G_sb[:, 4 + k, 0:512], bank(tb))),
            reads=[PB[tb]], writes=["G_sb"], name=f"evGT{k}")
    if dbg:
        s.dma("sp", dbg_G.rearrange("(a p) c -> p a c", p=128),
              G_sb, reads=["G_sb"], name="dbgG2")
    s.barrier("bar_st")
    for cj in range(8):
        s.add("dve", (lambda q, cj=cj: q.tensor_copy(
            G_bf[:, cj, :], G_sb[:, cj, :])),
            reads=["G_sb"], writes=["G_bf"], name=f"castG{cj}")

    for cj in range(8):
        s.add("pe", (lambda q, cj=cj: q.transpose(
            bank(5)[:, cj:cj + 1], u_sb[:, cj * 128:(cj + 1) * 128],
            ident_f32[0:1, 0:1])),
            reads=["u_sb", "ident_f32"], writes=[PB[5]], name=f"trU{cj}")
    s.add("dve", lambda q: q.tensor_copy(uT_f, bank(5)[:, 0:8]),
          reads=[PB[5]], writes=["uT_f"], name="evUT")
    s.add("dve", lambda q: q.tensor_copy(uT_bf, uT_f),
          reads=["uT_f"], writes=["uT_bf"], name="castUT")

    for dj in range(24):
        for cj in range(8):
            s.add("pe", (lambda q, dj=dj, cj=cj: q.matmul(
                bank(6)[:, dj:dj + 1],
                WqkvT[:, cj, dj * 128:(dj + 1) * 128],
                uT_bf[:, cj:cj + 1], start=(cj == 0), stop=(cj == 7))),
                reads=["WqkvT", "uT_bf"], writes=[PB[6]], name=f"mS{dj}_{cj}")
    s.add("dve", lambda q: q.tensor_copy(Scol, bank(6)[:, 0:24]),
          reads=[PB[6]], writes=["Scol"], name="evS")

    for dj in range(24):
        b0 = (dj % 2) * 2
        for cj in range(8):
            for h in range(2):
                s.add("pe", (lambda q, dj=dj, cj=cj, h=h, b0=b0: q.matmul(
                    bank(b0 + h), WqkvT[:, cj, dj * 128:(dj + 1) * 128],
                    G_bf[:, cj, h * 512:(h + 1) * 512],
                    start=(cj == 0), stop=(cj == 7))),
                    reads=["WqkvT", "G_bf"], writes=[PB[b0 + h]],
                    name=f"mM{dj}_{cj}_{h}")
        s.add("dve", (lambda q, dj=dj, b0=b0: q.tensor_tensor(
            dummy1k, pall[:, b0 * 512:(b0 + 2) * 512],
            Wm_tm[:, dj, :], ALU.mult)),
            reads=[PB[b0], PB[b0 + 1], "Wm_tm"], writes=["dummy1k"],
            name=f"qm{dj}")
        s.add("dve", (lambda q, dj=dj: q.tensor_reduce(
            Qcol[:, dj:dj + 1], dummy1k, AX.X, ALU.add)),
            reads=["dummy1k"], writes=["Qc"], name=f"qd{dj}")

    s.add("pool", lambda q: q.tensor_scalar(
        meanF, Scol, 1.0 / (NT * 128), None, ALU.mult),
        reads=["Scol"], writes=["meanF"], name="bn_m")
    s.add("pool", lambda q: q.tensor_scalar(
        vepsF, Qcol, 1.0 / (NT * 128), EPS, ALU.mult, ALU.add),
        reads=["Qc"], writes=["vepsF"], name="bn_q")
    s.add("pool", lambda q: q.tensor_tensor(d24, meanF, meanF, ALU.mult),
          reads=["meanF"], writes=["d24"], name="bn_mm")
    s.add("pool", lambda q: q.tensor_tensor(vepsF, vepsF, d24, ALU.subtract),
          reads=["vepsF", "d24"], writes=["vepsF"], name="bn_v")
    s.add("pool", lambda q: q.tensor_tensor(
        vepsF, vepsF, neghalf[:, 0:24], ALU.pow),
        reads=["vepsF", "neghalf"], writes=["vepsF"], name="bn_r")
    s.add("pool", lambda q: q.tensor_tensor(abn, vepsF, gammaF, ALU.mult),
          reads=["vepsF", "gbF"], writes=["abn"], name="bn_a")
    s.add("pool", lambda q: q.tensor_tensor(d24, meanF, abn, ALU.mult),
          reads=["meanF", "abn"], writes=["d24"], name="bn_ma")
    s.add("pool", lambda q: q.tensor_tensor(bbn, betaF, d24, ALU.subtract),
          reads=["gbF", "d24"], writes=["bbn"], name="bn_b")
    if dbg:
        s.dma("sp", dbg_abn, abn, reads=["abn"], name="dbgabn")
        s.dma("sp", dbg_bbn, bbn, reads=["bbn"], name="dbgbbn")
        s.dma("sp", dbg_u, u_sb, reads=["u_sb"], name="dbgu")
        s.dma("sp", dbg_S, Scol, reads=["Scol"], name="dbgS")
        s.dma("sp", dbg_Q, Qcol, reads=["Qc"], name="dbgQ")
        s.dma("sp", dbg_uT, uT_f, reads=["uT_f"], name="dbguT")

    if dbg:
        dump_wq(dbg_wqD, "D")
    s.barrier("bar_1b")
    def emit_ld1b(st):
        s.dma("sp", ln1b, lnx_scr[st * 512:(st + 1) * 512, :].rearrange(
            "(a p) c -> p a c", p=128),
            reads=[f"lnxs.{4 * st + i}" for i in range(4)],
            writes=["R_xin0", "R_xin1", "R_xin2", "R_xin3"], name=f"ld1b{st}")

    emit_ld1b(0)
    for st in range(8):
        lp = st % 2
        for cj in range(8):
            tb = 2 + (cj % 2)
            for i in range(4):
                s.add("pe", (lambda q, i=i, cj=cj, tb=tb: q.transpose(
                    bank_bf(tb)[:, i * 128:(i + 1) * 128],
                    ln1b[:, i, cj * 128:(cj + 1) * 128], ident_bf)),
                    reads=["R_xin0", "R_xin1", "R_xin2", "R_xin3", "ident_bf"],
                    writes=[PB[tb]], name=f"t1b{st}_{cj}_{i}")
            s.add("dve", (lambda q, cj=cj, lp=lp, tb=tb: q.tensor_copy(
                lnfm[:, lp, cj, :], bank_bf(tb)[:, 0:512])),
                reads=[PB[tb]], writes=[f"lnfm{lp}"], name=f"e1b{st}_{cj}")
        if st + 1 < 8:
            emit_ld1b(st + 1)
        if dbg and st == 0:
            for cj_ in range(8):
                s.add("dve", (lambda q, cj_=cj_: q.tensor_copy(
                    dummy1k[:, 0:512], lnfm[:, 0, cj_, :])),
                    reads=["lnfm0"], writes=["dummy1k"], name="dbgclf")
                s.dma("sp", dbg_lnfm[:, cj_, :], dummy1k[:, 0:512],
                      reads=["dummy1k"], writes=["dbgdummy"], name="dbglf")
        for g in range(6):
            sp_ = g % 2
            for djl in range(4):
                dj = g * 4 + djl
                bi = dj % 2
                for cj in range(8):
                    s.add("pe", (lambda q, dj=dj, cj=cj, bi=bi, lp=lp: q.matmul(
                        bank(bi), WqkvT[:, cj, dj * 128:(dj + 1) * 128],
                        lnfm[:, lp, cj, :], start=(cj == 0), stop=(cj == 7))),
                        reads=["WqkvT", f"lnfm{lp}"], writes=[PB[bi]],
                        name=f"qkv{st}_{dj}_{cj}")
                if dbg and st == 0 and dj == 0:
                    s.add("dve", (lambda q: q.tensor_copy(dummy1k[:, 0:512],
                                                          bank(0))),
                          reads=[PB[0]], writes=["dummy1k"], name="dbgcps")
                    s.dma("sp", dbg_ps, dummy1k[:, 0:512],
                          reads=["dummy1k"], writes=["dbgdummy"], name="dbgps")
                    for cj_ in range(8):
                        s.add("dve", (lambda q, cj_=cj_: q.tensor_copy(
                            dummy1k[:, 0:128], WqkvT[:, cj_, 0:128])),
                            reads=["WqkvT"], writes=["dummy1k"], name="dbgcw2")
                        s.dma("sp", dbg_wq2[:, cj_, :], dummy1k[:, 0:128],
                              reads=["dummy1k"], writes=["dbgdummy"], name="dbgw2")
                        s.add("dve", (lambda q, cj_=cj_: q.tensor_copy(
                            dummy1k[:, 0:512], lnfm[:, 0, cj_, :])),
                            reads=["lnfm0"], writes=["dummy1k"], name="dbgcl2")
                        s.dma("sp", dbg_lf2[:, cj_, :], dummy1k[:, 0:512],
                              reads=["dummy1k"], writes=["dbgdummy"], name="dbgl2")
                s.add("dve", (lambda q, djl=djl, bi=bi, sp_=sp_: q.tensor_copy(
                    stg[:, sp_, djl, :], bank(bi))),
                    reads=[PB[bi]], writes=[f"stg{sp_}"], name=f"eqkv{st}_{dj}")
            if dbg and st == 0 and g == 0:
                s.dma("sp", dbg_stg, stg[:, 0, :, :],
                      reads=["stg0"], writes=["dbgdummy"], name="dbgstg")
            s.dma("sp", xa_scr[g * 512:(g + 1) * 512,
                               st * 512:(st + 1) * 512].rearrange(
                "(a p) t -> p a t", p=128), stg[:, sp_, :, :],
                reads=[f"stg{sp_}"], writes=[f"xa.{g}.{st}"],
                name=f"spxa{st}_{g}")

    s.barrier("bar_W2")
    for dj in range(8):
        p = dj % 2
        s.dma("sp", wstg2[:, p, :], W_proj[dj * 128:(dj + 1) * 128, :],
              writes=[f"R_S"], name=f"ldWp{dj}")
        s.add("dve", (lambda q, p=p: q.tensor_copy(
            S_bf[:, 7168:8192], wstg2[:, p, :])),
            reads=["R_S"], writes=["R_S"], name=f"cbWp{dj}")
        tb = 2 + (dj % 2)
        for cj in range(8):
            s.add("pe", (lambda q, cj=cj, tb=tb: q.transpose(
                bank_bf(tb)[:, cj * 128:(cj + 1) * 128],
                S_bf[:, 7168 + cj * 128:7168 + (cj + 1) * 128], ident_bf)),
                reads=["R_S", "ident_bf"], writes=[PB[tb]],
                name=f"trWp{dj}_{cj}")
        s.add("dve", (lambda q, dj=dj, tb=tb: q.tensor_copy(
            WpT[:, :, dj * 128:(dj + 1) * 128],
            bank_bf(tb).rearrange("p (a b) -> p a b", a=8, b=128))),
            reads=[PB[tb]], writes=["WpT"], name=f"evWp{dj}")
    for i in range(8):
        p = i % 2
        s.dma("sp", wstg2[:, p, 0:64], pA[i * 128:(i + 1) * 128, :],
              writes=["R_S"], name=f"ldpA{i}")
        s.add("pe", (lambda q, i=i, p=p: q.transpose(
            bank(6)[0:64, (i % 4) * 128:(i % 4) * 128 + 128],
            wstg2[:, p, 0:64], ident_f32)),
            reads=["R_S", "ident_f32"], writes=[PB[6]], name=f"trpA{i}")
        s.add("dve", (lambda q, i=i: q.tensor_copy(
            pAT[:, i * 128:(i + 1) * 128],
            bank(6)[0:64, (i % 4) * 128:(i % 4) * 128 + 128])),
            reads=[PB[6]], writes=["R_S"], name=f"evpA{i}")
    s.add("dve", lambda q: q.tensor_copy(pATb, pAT),
          reads=["R_S"], writes=["R_S"], name="cpATb")
    s.dma("sp", wstg2[0:64, 0, :], pB, writes=["R_S"], name="ldpB")
    s.add("dve", lambda q: q.tensor_copy(pBb, wstg2[0:64, 0, :]),
          reads=["R_S"], writes=["R_S"], name="cpBb")
    for cj in range(8):
        b0 = (cj % 2) * 2
        for h in range(2):
            s.add("pe", (lambda q, cj=cj, h=h, b0=b0: q.matmul(
                bank(b0 + h), pBb[:, cj * 128:(cj + 1) * 128],
                pATb[:, h * 512:(h + 1) * 512], start=True, stop=True)),
                reads=["R_S"], writes=[PB[b0 + h]], name=f"mmPL{cj}_{h}")
        s.add("dve", (lambda q, cj=cj, b0=b0: q.tensor_tensor(
            WmpT[:, cj, :], pall[:, b0 * 512:(b0 + 2) * 512],
            WpT[:, cj, :], ALU.add)),
            reads=[PB[b0], PB[b0 + 1], "WpT"], writes=["WmpT"],
            name=f"evPL{cj}")

    s.barrier("bar_P2")
    for cj in range(8):
        s.dma("sp", Kfm[:, cj, :], xa_scr[C + cj * 128:C + (cj + 1) * 128, :],
              reads=[f"xa.{2 + cj // 4}.{st_}" for st_ in range(8)],
              writes=["Kfm"], name=f"ldK{cj}")
        s.add("dve", (lambda q, cj=cj: q.tensor_scalar(
            Kfm[:, cj, :], Kfm[:, cj, :], abn[:, 8 + cj:9 + cj],
            bbn[:, 8 + cj:9 + cj], ALU.mult, ALU.add)),
            reads=["Kfm", "abn", "bbn"], writes=["Kfm"], name=f"bnK{cj}")
    VBN_RES = ["R_qp", "R_XT", "qfm_r", "yfm_r"]
    for sp_ in range(4):
        for cj in range(8):
            s.dma("sp", vbn[:, cj, :],
                  xa_scr[2 * C + cj * 128:2 * C + (cj + 1) * 128,
                         sp_ * 1024:(sp_ + 1) * 1024],
                  reads=[f"xa.{4 + cj // 4}.{2 * sp_}",
                         f"xa.{4 + cj // 4}.{2 * sp_ + 1}"],
                  writes=VBN_RES, name=f"ldV{sp_}_{cj}")
            s.add("dve", (lambda q, cj=cj: q.tensor_scalar(
                vbn[:, cj, :], vbn[:, cj, :], abn[:, 16 + cj:17 + cj],
                bbn[:, 16 + cj:17 + cj], ALU.mult, ALU.add)),
                reads=VBN_RES + ["abn", "bbn"], writes=VBN_RES,
                name=f"bnV{sp_}_{cj}")
        for sk in range(8):
            tb = sk % 2
            for cj in range(8):
                s.add("pe", (lambda q, sk=sk, cj=cj, tb=tb: q.transpose(
                    bank_bf(tb)[:, cj * 128:(cj + 1) * 128],
                    vbn[:, cj, sk * 128:(sk + 1) * 128], ident_bf)),
                    reads=VBN_RES + ["ident_bf"], writes=[PB[tb]],
                    name=f"trV{sp_}_{sk}_{cj}")
            s.add("dve", (lambda q, sp_=sp_, sk=sk, tb=tb: q.tensor_copy(
                Vtm[:, sp_ * 8 + sk, :], bank_bf(tb))),
                reads=[PB[tb]], writes=["Vtm"], name=f"evV{sp_}_{sk}")

    s.barrier("bar_P3")
    c_mx = col(); c_negm = col(); c_lt = col(); c_linv = col()
    c_lnli = col(); c_b2 = col()
    c_s2 = col(); c_q2 = col(); c_m2 = col(); c_r2 = col()
    c_mr2 = col(); c_t2a = col(); c_t2b = col()

    def p3_scores_early(qt):
        nk = qt + 1
        w_all = nk * 128
        nb = (w_all + 511) // 512
        if qt == 0:
            s.dma("sp", qpan, xa_scr[0:C, 0:128].rearrange(
                "(a p) t -> p a t", p=128),
                reads=["xa.0.0", "xa.1.0"], writes=["R_qp"], name="ldq0")
        for cj in range(8):
            s.add("dve", (lambda q, cj=cj: q.tensor_scalar(
                qfm[:, cj, :], qpan[:, cj, :],
                abn[:, cj:cj + 1], bbn[:, cj:cj + 1], ALU.mult, ALU.add)),
                reads=["R_qp", "abn", "bbn"], writes=["qfm_r"],
                name=f"bnq{qt}_{cj}")
        if qt + 1 < 32:
            s.dma("sp", qpan,
                  xa_scr[0:C, (qt + 1) * 128:(qt + 2) * 128].rearrange(
                      "(a p) t -> p a t", p=128),
                  reads=[f"xa.0.{(qt + 1) // 4}", f"xa.1.{(qt + 1) // 4}"],
                  writes=["R_qp"], name=f"ldq{qt + 1}")
        for blk in range(min(nb, 2)):
            w = min(512, w_all - blk * 512)
            for cj in range(8):
                s.add("pe", (lambda q, blk=blk, w=w, cj=cj: q.matmul(
                    bank(blk)[:, 0:w], qfm[:, cj, :],
                    Kfm[:, cj, blk * 512:blk * 512 + w],
                    start=(cj == 0), stop=(cj == 7))),
                    reads=["qfm_r", "Kfm"], writes=[PB[blk]],
                    name=f"sc{qt}_{blk}_{cj}")

    def p3_scores_rest(qt):
        nk = qt + 1
        w_all = nk * 128
        nb = (w_all + 511) // 512
        blk_has_body = []
        for blk in range(nb):
            w = min(512, w_all - blk * 512)
            bi = blk % 2
            if blk >= 2:
                for cj in range(8):
                    s.add("pe", (lambda q, blk=blk, w=w, bi=bi, cj=cj:
                                 q.matmul(
                        bank(bi)[:, 0:w], qfm[:, cj, :],
                        Kfm[:, cj, blk * 512:blk * 512 + w],
                        start=(cj == 0), stop=(cj == 7))),
                        reads=["qfm_r", "Kfm"], writes=[PB[bi]],
                        name=f"sc{qt}_{blk}_{cj}")
            wd = w - 128 if blk == nb - 1 else w
            blk_has_body.append(wd > 0)
            if wd > 0:
                s.add("dve", (lambda q, blk=blk, wd=wd, bi=bi: q.tensor_scalar(
                    S_f[:, blk * 512:blk * 512 + wd], bank(bi)[:, 0:wd],
                    0.0, -3e38, ALU.add, ALU.max,
                    accum_out=gmaxs[:, blk:blk + 1])),
                    reads=[PB[bi]], writes=["R_S"], accum=[f"gmx{blk}"],
                    name=f"evs{qt}_{blk}")
            if blk == nb - 1:
                s.add("dve", (lambda q, bi=bi, wd=wd, w_all=w_all:
                              q.tensor_tensor(
                    S_f[:, w_all - 128:w_all], bank(bi)[:, wd:wd + 128],
                    maskc, ALU.add)),
                    reads=[PB[bi], "maskc"], writes=["R_S"], name=f"msk{qt}")
                s.add("dve", (lambda q, w_all=w_all, nb=nb: q.tensor_reduce(
                    gmaxs[:, nb:nb + 1], S_f[:, w_all - 128:w_all],
                    AX.X, ALU.max)),
                    reads=["R_S"], writes=["gmxd"], name=f"dmx{qt}")
        s.add("pool", (lambda q, nb=nb: q.tensor_scalar(
            c_mx, gmaxs[:, nb:nb + 1], -3e38, None, ALU.max)),
            reads=["gmxd"], writes=["c_mx"], name=f"fm0{qt}")
        for b in range(nb):
            if not blk_has_body[b]:
                continue
            s.add("pool", (lambda q, b=b: q.tensor_scalar(
                c_mx, c_mx, gmaxs[:, b:b + 1], None, ALU.max)),
                reads=["c_mx", f"gmx{b}"], writes=["c_mx"], name=f"fm{qt}_{b}")
        s.add("pool", lambda q: q.tensor_scalar(
            c_negm, c_mx, -0.03125, None, ALU.mult),
            reads=["c_mx"], writes=["c_negm"], name=f"ngm{qt}")

    def p3_soft(qt):
        nk = qt + 1
        w_all = nk * 128
        nb = (w_all + 511) // 512
        for blk in range(nb):
            w = min(512, w_all - blk * 512)
            s.add("act", (lambda q, blk=blk, w=w: q.activation(
                Pscr[:, blk % 2, 0:w], S_f[:, blk * 512:blk * 512 + w],
                AF.Exp, bias=c_negm, scale=0.03125,
                accum_out=lsums[:, blk:blk + 1])),
                reads=["R_S", "c_negm", f"Ps{blk % 2}"],
                writes=[f"Ps{blk % 2}"], accum=[f"lsm{blk}"],
                name=f"exp{qt}_{blk}")
        s.add("dve", (lambda q, nb=nb: q.tensor_reduce(
            c_lt, lsums[:, 0:nb], AX.X, ALU.add)),
            reads=[f"lsm{b}" for b in range(nb)], writes=["c_lt"],
            name=f"lt{qt}")
        s.add("dve", lambda q: q.reciprocal(c_linv, c_lt),
              reads=["c_lt"], writes=["c_linv"], name=f"li{qt}")
        s.add("act", lambda q: q.activation(
            c_lnli, c_linv, AF.Ln),
            reads=["c_linv"], writes=["c_lnli"], name=f"lnl{qt}")
        s.add("pool", lambda q: q.tensor_tensor(
            c_b2, c_negm, c_lnli, ALU.add),
            reads=["c_negm", "c_lnli"], writes=["c_b2"], name=f"b2{qt}")
        for blk in range(nb - 1, -1, -1):
            w = min(512, w_all - blk * 512)
            pp = blk % 2
            s.add("act", (lambda q, blk=blk, w=w, pp=pp: q.activation(
                Pscr[:, pp, 0:w], S_f[:, blk * 512:blk * 512 + w],
                AF.Exp, bias=c_b2, scale=0.03125)),
                reads=["R_S", "c_b2", f"Ps{pp}"], writes=[f"Ps{pp}"],
                name=f"exq{qt}_{blk}")
            nch = (w + 127) // 128
            for j in range(nch):
                s.add("pe", (lambda q, j=j, pp=pp: q.transpose(
                    bank_bf(2)[:, j * 128:(j + 1) * 128],
                    Pscr[:, pp, j * 128:(j + 1) * 128], ident_bf)),
                    reads=[f"Ps{pp}", "ident_bf"], writes=[PB[2]],
                    name=f"tp{qt}_{blk}_{j}")
            s.add("dve", (lambda q, blk=blk, nch=nch: q.tensor_copy(
                S_bf[:, 4096 + blk * 512:4096 + blk * 512 + nch * 128],
                bank_bf(2)[:, 0:nch * 128])),
                reads=[PB[2]], writes=["R_S"], name=f"etp{qt}_{blk}")
            for j in range(nch - 1, -1, -1):
                sk = blk * 4 + j
                for h in range(2):
                    s.add("pe", (lambda q, sk=sk, h=h, nk=nk: q.matmul(
                        bank(3 + h),
                        S_bf[:, 4096 + sk * 128:4096 + (sk + 1) * 128],
                        Vtm[:, sk, h * 512:(h + 1) * 512],
                        start=(sk == nk - 1), stop=(sk == 0))),
                        reads=["Vtm", "R_S"], writes=[PB[3 + h]],
                        name=f"pv{qt}_{sk}_{h}")
        s.add("dve", lambda q: q.tensor_copy(
            ytm, pall[:, 3 * 512:5 * 512]),
            reads=[PB[3], PB[4]], writes=["ytm_r"], name=f"evyt{qt}")
        for cj in range(8):
            s.add("pe", (lambda q, cj=cj: q.transpose(
                bank_bf(2)[:, cj * 128:(cj + 1) * 128],
                ytm[:, cj * 128:(cj + 1) * 128], ident_bf)),
                reads=["ytm_r", "ident_bf"], writes=[PB[2]], name=f"ty{qt}_{cj}")
        s.add("dve", lambda q: q.tensor_copy(
            yfm.rearrange("p a b -> p (a b)"), bank_bf(2)),
            reads=[PB[2]], writes=["yfm_r"], name=f"evy{qt}")
        for cj in range(8):
            for h in range(2):
                s.add("pe", (lambda q, cj=cj, h=h: q.matmul(
                    bank(5 + h), yfm[:, cj, :],
                    WpT[:, cj, h * 512:(h + 1) * 512],
                    start=(cj == 0), stop=(cj == 7))),
                    reads=["WpT", "yfm_r"], writes=[PB[5 + h]],
                    name=f"p1{qt}_{cj}_{h}")
        s.add("dve", lambda q: q.tensor_copy(ztm, pall[:, 5 * 512:7 * 512]),
              reads=[PB[5], PB[6]], writes=["ztm_r"], name=f"evzt{qt}")
        for cj in range(8):
            s.add("pe", (lambda q, cj=cj: q.transpose(
                bank_bf(2)[:, cj * 128:(cj + 1) * 128],
                ztm[:, cj * 128:(cj + 1) * 128], ident_bf)),
                reads=["ztm_r", "ident_bf"], writes=[PB[2]], name=f"tz{qt}_{cj}")
        s.add("dve", lambda q: q.tensor_copy(
            zfm.rearrange("p a b -> p (a b)"), bank_bf(2)),
            reads=[PB[2]], writes=["zfm_r"], name=f"evz{qt}")
        for cj in range(8):
            for h in range(2):
                s.add("pe", (lambda q, cj=cj, h=h: q.matmul(
                    bank(3 + h), zfm[:, cj, :],
                    WmpT[:, cj, h * 512:(h + 1) * 512],
                    start=(cj == 0), stop=(cj == 7))),
                    reads=["WmpT", "zfm_r"], writes=[PB[3 + h]],
                    name=f"p2{qt}_{cj}_{h}")
        s.add("dve", lambda q: q.tensor_copy(wtm, pall[:, 3 * 512:5 * 512]),
              reads=[PB[3], PB[4]], writes=["R_WW"], name=f"evw{qt}")

    def p3_ln2(qt):
        s.dma("sp", XT, x_all[qt * 128:(qt + 1) * 128, :],
              writes=["R_XT"], name=f"ldxt{qt}")
        s.add("dve", lambda q: q.tensor_reduce(c_s2, wtm, AX.X, ALU.add),
              reads=["R_WW"], writes=["c_s2"], name=f"l2s{qt}")
        s.add("act", lambda q: q.activation(
            dummy1k, wtm, AF.Square, accum_out=c_q2),
            reads=["R_WW"], writes=["dummy1k"], accum=["c_q2"], name=f"l2q{qt}")
        s.add("pool", lambda q: q.tensor_scalar(
            c_m2, c_s2, 1.0 / C, None, ALU.mult),
            reads=["c_s2"], writes=["c_m2"], name=f"l2m{qt}")
        s.add("pool", lambda q: q.tensor_tensor(c_t2a, c_m2, c_m2, ALU.mult),
              reads=["c_m2"], writes=["c_t2a"], name=f"l2mm{qt}")
        s.add("pool", lambda q: q.tensor_scalar(
            c_t2b, c_q2, 1.0 / C, EPS, ALU.mult, ALU.add),
            reads=["c_q2"], writes=["c_t2b"], name=f"l2qq{qt}")
        s.add("pool", lambda q: q.tensor_tensor(
            c_t2b, c_t2b, c_t2a, ALU.subtract),
            reads=["c_t2b", "c_t2a"], writes=["c_t2b"], name=f"l2v{qt}")
        s.add("pool", lambda q: q.tensor_tensor(c_r2, c_t2b, nh_col, ALU.pow),
              reads=["c_t2b", "neghalf"], writes=["c_r2"], name=f"l2r{qt}")
        s.add("pool", lambda q: q.tensor_tensor(c_mr2, c_m2, c_r2, ALU.mult),
              reads=["c_m2", "c_r2"], writes=["c_mr2"], name=f"l2mr{qt}")
        s.add("dve", lambda q: q.scalar_tensor_tensor(
            out=XT, in0=wtm, scalar=c_r2, in1=XT, op0=ALU.mult, op1=ALU.add),
            reads=["R_WW", "c_r2", "R_XT"], writes=["R_XT"], name=f"res{qt}")
        s.add("dve", lambda q: q.tensor_scalar(
            XT, XT, c_mr2, None, ALU.subtract),
            reads=["R_XT", "c_mr2"], writes=["R_XT"], name=f"res2{qt}")
        s.dma("sp", xnew_scr[qt * 128:(qt + 1) * 128, :], XT,
              reads=["R_XT"], writes=[f"xn.{qt}"], name=f"stxn{qt}")

    p3_scores_early(0)
    p3_scores_rest(0)
    for qt in range(32):
        if qt + 1 < 32:
            p3_scores_early(qt + 1)
        p3_soft(qt)
        if qt + 1 < 32:
            p3_scores_rest(qt + 1)
        p3_ln2(qt)
    s.barrier("bar_4p")

    def ffn_w_prep_1024(widx, wsrc, wdst_ap, wdst_res, tag):
        for ic in range(32):
            p = ic % 2
            s.dma("sp", wstg4[:, p, :], wsrc[ic * 128:(ic + 1) * 128, :],
                  writes=[f"wst4_{p}"], name=f"mw{tag}{ic}")
            s.add("dve", (lambda q, ic=ic, p=p: q.tensor_reduce(
                absc[:, ic:ic + 1], wstg4[:, p, :], AX.X, ALU.add,
                apply_absolute_value=True)),
                reads=[f"wst4_{p}"], writes=["absc"], name=f"ma{tag}{ic}")
        s.add("dve", lambda q: q.tensor_reduce(
            gmax8[:, 0:1], absc[:, 0:32], AX.X, ALU.add),
            reads=["absc"], writes=["gmax8"], name=f"mr{tag}")
        s.add("pe", lambda q: q.matmul(
            bank(7)[0:1, 0:1], gmax8[:, 0:1], ones_f32[:, 0:1],
            start=True, stop=True),
            reads=["gmax8", "ones_f32"], writes=[PB[7]], name=f"mm{tag}")
        s.add("dve", lambda q: q.tensor_copy(ms1[:, 0:1], bank(7)[0:1, 0:1]),
              reads=[PB[7]], writes=["ms1a"], name=f"me{tag}")
        nel = float(32 * 128 * 1024)
        s.add("pool", (lambda q, nel=nel: q.tensor_scalar(
            ms1[:, 1:2], ms1[:, 0:1], 1.0 / nel, 1e-5, ALU.mult, ALU.max)),
            reads=["ms1a"], writes=["ms1b"], name=f"mc{tag}")
        s.add("dve", lambda q: q.reciprocal(ms1[:, 2:3], ms1[:, 1:2]),
              reads=["ms1b"], writes=["ms1c"], name=f"mi{tag}")
        s.add("pool", lambda q: q.tensor_scalar(
            ms1[:, 3:4], ms1[:, 1:2], 1.0 / 127.0, None, ALU.mult),
            reads=["ms1b"], writes=["ms1d"], name=f"md{tag}")
        s.add("pe", lambda q: q.matmul(
            bank(7)[:, 0:1], ones_f32[0:1, :], ms1[:, 2:3],
            start=True, stop=True),
            reads=["ms1c", "ones_f32"], writes=[PB[7]], name=f"bc1{tag}")
        s.add("pe", lambda q: q.matmul(
            bank(7)[:, 1:2], ones_f32[0:1, :], ms1[:, 3:4],
            start=True, stop=True),
            reads=["ms1d", "ones_f32"], writes=[PB[7]], name=f"bc2{tag}")
        s.add("dve", (lambda q, widx=widx: q.tensor_copy(
            swcol[:, widx:widx + 1], bank(7)[:, 0:1])),
            reads=[PB[7]], writes=["swcol"], name=f"esw{tag}")
        s.add("dve", (lambda q, widx=widx: q.tensor_copy(
            mwcol[:, widx:widx + 1], bank(7)[:, 1:2])),
            reads=[PB[7]], writes=["mwcol"], name=f"emw{tag}")
        for ic in range(32):
            p = ic % 2
            kqb = kq if p == 0 else rq4.bitcast(BF16)[:, 0:1024]
            kqres = "kq_r" if p == 0 else "rq4"
            s.dma("sp", wstg4[:, p, :], wsrc[ic * 128:(ic + 1) * 128, :],
                  writes=[f"wst4_{p}"], name=f"tw{tag}{ic}")
            s.add("dve", (lambda q, p=p, widx=widx: q.tensor_scalar(
                wstg4[:, p, :], wstg4[:, p, :], swcol[:, widx:widx + 1],
                MAGIC, ALU.mult, ALU.add)),
                reads=[f"wst4_{p}", "swcol"], writes=[f"wst4_{p}"],
                name=f"t1{tag}{ic}")
            s.add("act", (lambda q, p=p: q.activation(
                wstg4[:, p, :], wstg4[:, p, :], AF.Identity, bias=negmag)),
                reads=[f"wst4_{p}", "negmag"], writes=[f"wst4_{p}"],
                name=f"t2{tag}{ic}")
            s.add("dve", (lambda q, p=p, kqb=kqb: q.tensor_scalar(
                kqb, wstg4[:, p, :], 1.0, -1.0, ALU.min, ALU.max)),
                reads=[f"wst4_{p}"], writes=[kqres], name=f"t3{tag}{ic}")
            tb = 4 + (ic % 2)
            for cj in range(8):
                s.add("pe", (lambda q, cj=cj, tb=tb, kqb=kqb: q.transpose(
                    bank_bf(tb)[:, cj * 128:(cj + 1) * 128],
                    kqb[:, cj * 128:(cj + 1) * 128], ident_bf)),
                    reads=[kqres, "ident_bf"], writes=[PB[tb]],
                    name=f"t4{tag}{ic}_{cj}")
            s.add("dve", (lambda q, ic=ic, tb=tb: q.tensor_copy(
                wdst_ap[:, :, ic * 128:(ic + 1) * 128],
                bank_bf(tb).rearrange("p (a b) -> p a b", a=8, b=128))),
                reads=[PB[tb]], writes=[wdst_res], name=f"t5{tag}{ic}")

    ffn_w_prep_1024(0, w_gate, WgT, "WgT", "g")
    ffn_w_prep_1024(1, w_up, WuT, "WuT", "u")

    for dc in range(8):
        s.dma("sp", gu, w_down[dc * 128:(dc + 1) * 128, :],
              writes=["gu_r"], name=f"mwd{dc}")
        s.add("dve", (lambda q, dc=dc: q.tensor_reduce(
            absc[:, dc:dc + 1], gu, AX.X, ALU.add,
            apply_absolute_value=True)),
            reads=["gu_r"], writes=["absc"], name=f"mad{dc}")
    s.add("dve", lambda q: q.tensor_reduce(
        gmax8[:, 0:1], absc[:, 0:8], AX.X, ALU.add),
        reads=["absc"], writes=["gmax8"], name="mrd")
    s.add("pe", lambda q: q.matmul(
        bank(7)[0:1, 0:1], gmax8[:, 0:1], ones_f32[:, 0:1],
        start=True, stop=True),
        reads=["gmax8", "ones_f32"], writes=[PB[7]], name="mmd")
    s.add("dve", lambda q: q.tensor_copy(ms1[:, 0:1], bank(7)[0:1, 0:1]),
          reads=[PB[7]], writes=["ms1a"], name="med")
    s.add("pool", lambda q: q.tensor_scalar(
        ms1[:, 1:2], ms1[:, 0:1], 1.0 / float(C * 4 * C), 1e-5,
        ALU.mult, ALU.max),
        reads=["ms1a"], writes=["ms1b"], name="mcd")
    s.add("dve", lambda q: q.reciprocal(ms1[:, 2:3], ms1[:, 1:2]),
          reads=["ms1b"], writes=["ms1c"], name="mid")
    s.add("pool", lambda q: q.tensor_scalar(
        ms1[:, 3:4], ms1[:, 1:2], 1.0 / 127.0, None, ALU.mult),
        reads=["ms1b"], writes=["ms1d"], name="mdd")
    s.add("pe", lambda q: q.matmul(
        bank(7)[:, 0:1], ones_f32[0:1, :], ms1[:, 2:3], start=True, stop=True),
        reads=["ms1c", "ones_f32"], writes=[PB[7]], name="bc1d")
    s.add("pe", lambda q: q.matmul(
        bank(7)[:, 1:2], ones_f32[0:1, :], ms1[:, 3:4], start=True, stop=True),
        reads=["ms1d", "ones_f32"], writes=[PB[7]], name="bc2d")
    s.add("dve", lambda q: q.tensor_copy(swcol[:, 2:3], bank(7)[:, 0:1]),
          reads=[PB[7]], writes=["swcol"], name="eswd")
    s.add("dve", lambda q: q.tensor_copy(mwcol[:, 2:3], bank(7)[:, 1:2]),
          reads=[PB[7]], writes=["mwcol"], name="emwd")

    c3 = {k: [col(), col()] for k in ("s", "q", "m", "r", "am", "ami", "qs",
                                      "ag", "au", "am2", "am2c", "qs2",
                                      "t1", "t2", "t3")}

    def chain4a(t):
        p = t % 2
        s.dma("sp", xt4[:, p, :], xnew_scr[t * 128:(t + 1) * 128, :],
              reads=[f"xn.{t}"], writes=[f"xt4{p}"], name=f"ldx4{t}")
        s.add("dve", (lambda q, p=p: q.tensor_reduce(
            c3["s"][p], xt4[:, p, :], AX.X, ALU.add)),
            reads=[f"xt4{p}"], writes=[f"c3s{p}"], name=f"l3s{t}")
        s.add("act", (lambda q, p=p: q.activation(
            dummy1k, xt4[:, p, :], AF.Square, accum_out=c3["q"][p])),
            reads=[f"xt4{p}"], writes=["dummy1k"], accum=[f"c3q{p}"],
            name=f"l3q{t}")
        s.add("pool", (lambda q, p=p: q.tensor_scalar(
            c3["m"][p], c3["s"][p], 1.0 / C, None, ALU.mult)),
            reads=[f"c3s{p}"], writes=[f"c3m{p}"], name=f"l3m{t}")
        s.add("pool", (lambda q, p=p: q.tensor_tensor(
            c3["t1"][p], c3["m"][p], c3["m"][p], ALU.mult)),
            reads=[f"c3m{p}"], writes=[f"c3t1{p}"], name=f"l3mm{t}")
        s.add("pool", (lambda q, p=p: q.tensor_scalar(
            c3["t2"][p], c3["q"][p], 1.0 / C, EPS, ALU.mult, ALU.add)),
            reads=[f"c3q{p}"], writes=[f"c3t2{p}"], name=f"l3qq{t}")
        s.add("pool", (lambda q, p=p: q.tensor_tensor(
            c3["t2"][p], c3["t2"][p], c3["t1"][p], ALU.subtract)),
            reads=[f"c3t2{p}", f"c3t1{p}"], writes=[f"c3t2{p}"], name=f"l3v{t}")
        s.add("pool", (lambda q, p=p: q.tensor_tensor(
            c3["r"][p], c3["t2"][p], nh_col, ALU.pow)),
            reads=[f"c3t2{p}", "neghalf"], writes=[f"c3r{p}"], name=f"l3r{t}")
        s.add("dve", (lambda q, p=p: q.tensor_scalar(
            ln3x, xt4[:, p, :], c3["m"][p], c3["r"][p], ALU.subtract,
            ALU.mult)),
            reads=[f"xt4{p}", f"c3m{p}", f"c3r{p}"], writes=["ln3x"],
            name=f"l3n{t}")
        s.add("dve", (lambda q, p=p: q.tensor_reduce(
            c3["am"][p], ln3x, AX.X, ALU.max, apply_absolute_value=True)),
            reads=["ln3x"], writes=[f"c3am{p}"], name=f"am{t}")
        s.add("dve", (lambda q, p=p: q.tensor_scalar(
            c3["am"][p], c3["am"][p], 1e-5, None, ALU.max)),
            reads=[f"c3am{p}"], writes=[f"c3am{p}"], name=f"amc{t}")
        s.add("dve", (lambda q, p=p: q.reciprocal(c3["ami"][p], c3["am"][p])),
              reads=[f"c3am{p}"], writes=[f"c3ami{p}"], name=f"ami{t}")
        s.add("dve", (lambda q, p=p: q.tensor_scalar(
            c3["qs"][p], c3["ami"][p], 127.0, None, ALU.mult)),
            reads=[f"c3ami{p}"], writes=[f"c3qs{p}"], name=f"qs{t}")
        s.add("pool", (lambda q, p=p: q.tensor_tensor(
            c3["ag"][p], c3["am"][p], mwcol[:, 0:1], ALU.mult)),
            reads=[f"c3am{p}", "mwcol"], writes=[f"c3ag{p}"], name=f"ag{t}")
        s.add("pool", (lambda q, p=p: q.tensor_tensor(
            c3["au"][p], c3["am"][p], mwcol[:, 1:2], ALU.mult)),
            reads=[f"c3am{p}", "mwcol"], writes=[f"c3au{p}"], name=f"au{t}")
        s.add("dve", (lambda q, p=p: q.tensor_scalar(
            rq4, ln3x, c3["qs"][p], MAGIC, ALU.mult, ALU.add)),
            reads=["ln3x", f"c3qs{p}"], writes=["rq4"], name=f"rqo{t}")
        s.add("dve", lambda q: q.tensor_scalar(
            kq, rq4, MAGIC, None, ALU.subtract),
            reads=["rq4"], writes=["kq_r"], name=f"kqo{t}")

    def transp4a(t):
        p = t % 2
        for cj in range(8):
            s.add("pe", (lambda q, cj=cj: q.transpose(
                bank_bf(6)[:, cj * 128:(cj + 1) * 128],
                kq[:, cj * 128:(cj + 1) * 128], ident_bf)),
                reads=["kq_r", "ident_bf"], writes=[PB[6]], name=f"tkq{t}_{cj}")
        s.add("dve", (lambda q, p=p: q.tensor_copy(
            kqT[:, p, :, :], bank_bf(6).rearrange("x (a b) -> x a b",
                                                  a=8, b=128))),
            reads=[PB[6]], writes=[f"kqT{p}"], name=f"ekq{t}")

    def mm4a(t):
        p = t % 2
        for blk in range(8):
            gb = (blk % 3) * 2
            ub = gb + 1
            for cj in range(8):
                s.add("pe", (lambda q, blk=blk, cj=cj, gb=gb, p=p: q.matmul(
                    bank(gb), kqT[:, p, cj, :],
                    WgT[:, cj, blk * 512:(blk + 1) * 512],
                    start=(cj == 0), stop=(cj == 7))),
                    reads=[f"kqT{p}", "WgT"], writes=[PB[gb]],
                    name=f"gg{t}_{blk}_{cj}")
            for cj in range(8):
                s.add("pe", (lambda q, blk=blk, cj=cj, ub=ub, p=p: q.matmul(
                    bank(ub), kqT[:, p, cj, :],
                    WuT[:, cj, blk * 512:(blk + 1) * 512],
                    start=(cj == 0), stop=(cj == 7))),
                    reads=[f"kqT{p}", "WuT"], writes=[PB[ub]],
                    name=f"guu{t}_{blk}_{cj}")
            sp_ = blk % 2
            s.add("act", (lambda q, gb=gb, sp_=sp_, p=p: q.activation(
                sg4[:, sp_, :], bank(gb), AF.Silu, scale=c3["ag"][p])),
                reads=[PB[gb], f"c3ag{p}", f"sg{sp_}"], writes=[f"sg{sp_}"],
                name=f"si{t}_{blk}")
            s.add("dve", (lambda q, blk=blk, ub=ub, sp_=sp_, p=p:
                          q.scalar_tensor_tensor(
                out=gu[:, blk * 512:(blk + 1) * 512], in0=bank(ub),
                scalar=c3["au"][p], in1=sg4[:, sp_, :],
                op0=ALU.mult, op1=ALU.mult)),
                reads=[PB[ub], f"c3au{p}", f"sg{sp_}"], writes=["gu_r"],
                name=f"gm{t}_{blk}")
            s.add("dve", (lambda q, blk=blk: q.tensor_reduce(
                gmax8[:, blk:blk + 1], gu[:, blk * 512:(blk + 1) * 512],
                AX.X, ALU.max, apply_absolute_value=True)),
                reads=["gu_r"], writes=["gmax8"], name=f"gx{t}_{blk}")

    def tail4a(t):
        p = t % 2
        s.add("dve", (lambda q, p=p: q.tensor_reduce(
            c3["am2"][p], gmax8[:, 0:8], AX.X, ALU.max)),
            reads=["gmax8"], writes=[f"c3am2{p}"], name=f"am2{t}")
        s.add("dve", (lambda q, p=p: q.tensor_scalar(
            c3["am2c"][p], c3["am2"][p], 1e-5, None, ALU.max)),
            reads=[f"c3am2{p}"], writes=[f"c3am2c{p}"], name=f"am2c{t}")
        s.add("dve", (lambda q, p=p: q.reciprocal(c3["t3"][p], c3["am2c"][p])),
              reads=[f"c3am2c{p}"], writes=[f"c3t3{p}"], name=f"am2i{t}")
        s.add("dve", (lambda q, p=p: q.tensor_scalar(
            c3["qs2"][p], c3["t3"][p], 127.0, None, ALU.mult)),
            reads=[f"c3t3{p}"], writes=[f"c3qs2{p}"], name=f"qs2{t}")
        s.add("pool", (lambda q, t=t, p=p: q.tensor_tensor(
            adal[:, t:t + 1], c3["am2c"][p], mwcol[:, 2:3], ALU.mult)),
            reads=[f"c3am2c{p}", "mwcol"], writes=["adal"], name=f"ad{t}")
        s.add("dve", (lambda q, p=p: q.tensor_scalar(
            gu, gu, c3["qs2"][p], MAGIC, ALU.mult, ALU.add)),
            reads=["gu_r", f"c3qs2{p}"], writes=["gu_r"], name=f"gq1{t}")
        s.add("dve", lambda q: q.tensor_scalar(
            gq, gu, MAGIC, None, ALU.subtract),
            reads=["gu_r"], writes=["gq_r"], name=f"gq2{t}")
        s.dma("sp", gq_scr[t * 128:(t + 1) * 128, :], gq,
              reads=["gq_r"], writes=[f"gqs.{t}"], name=f"stgq{t}")

    chain4a(0)
    transp4a(0)
    for t in range(NTO):
        if t + 1 < NTO:
            chain4a(t + 1)
        mm4a(t)
        if t + 1 < NTO:
            transp4a(t + 1)
        tail4a(t)

    s.barrier("bar_4b")
    for dc in range(8):
        p = dc % 2
        s.dma("sp", wstg4b[:, p, :], w_down[dc * 128:(dc + 1) * 128, :],
              writes=[f"wstb{p}"], name=f"twd{dc}")
        s.add("dve", (lambda q, p=p: q.tensor_scalar(
            wstg4b[:, p, :], wstg4b[:, p, :], swcol[:, 2:3], MAGIC,
            ALU.mult, ALU.add)),
            reads=[f"wstb{p}", "swcol"], writes=[f"wstb{p}"], name=f"td1{dc}")
        s.add("dve", (lambda q, p=p: q.tensor_scalar(
            wstg4b[:, p, :], wstg4b[:, p, :], MAGIC, None, ALU.subtract)),
            reads=[f"wstb{p}"], writes=[f"wstb{p}"], name=f"td2{dc}")
        s.add("dve", (lambda q, p=p: q.tensor_scalar(
            tern4b, wstg4b[:, p, :], 1.0, -1.0, ALU.min, ALU.max)),
            reads=[f"wstb{p}"], writes=["tern4b"], name=f"td3{dc}")
        for g4 in range(4):
            tb = 4 + (g4 % 2)
            for j in range(8):
                ic = g4 * 8 + j
                s.add("pe", (lambda q, ic=ic, j=j, tb=tb: q.transpose(
                    bank_bf(tb)[:, j * 128:(j + 1) * 128],
                    tern4b[:, ic * 128:(ic + 1) * 128], ident_bf)),
                    reads=["tern4b", "ident_bf"], writes=[PB[tb]],
                    name=f"td4{dc}_{ic}")
            s.add("dve", (lambda q, dc=dc, g4=g4, tb=tb: q.tensor_copy(
                WdT[:, g4 * 8:(g4 + 1) * 8, dc * 128:(dc + 1) * 128],
                bank_bf(tb).rearrange("p (a b) -> p a b", a=8, b=128))),
                reads=[PB[tb]], writes=["WdT"], name=f"td5{dc}_{g4}")

    def transp4b(t):
        p = t % 2
        for g4 in range(4):
            tb = 4 + (g4 % 2)
            for j in range(8):
                ic = g4 * 8 + j
                s.add("pe", (lambda q, ic=ic, j=j, p=p, tb=tb: q.transpose(
                    bank_bf(tb)[:, j * 128:(j + 1) * 128],
                    gq4[:, p, ic * 128:(ic + 1) * 128], ident_bf)),
                    reads=[f"gq4{p}", "ident_bf"], writes=[PB[tb]],
                    name=f"tgq{t}_{ic}")
            s.add("dve", (lambda q, g4=g4, tb=tb, p=p: q.tensor_copy(
                gqT[:, p, g4 * 8:(g4 + 1) * 8, :].rearrange(
                    "x a b -> x (a b)"),
                bank_bf(tb))),
                reads=[PB[tb]], writes=[f"gqT{p}"], name=f"egq{t}_{g4}")

    def mm4b(t):
        p = t % 2
        for ds in range(2):
            for ic in range(32):
                s.add("pe", (lambda q, ds=ds, ic=ic, p=p: q.matmul(
                    bank(ds), gqT[:, p, ic, :],
                    WdT[:, ic, ds * 512:(ds + 1) * 512],
                    start=(ic == 0), stop=(ic == 31))),
                    reads=[f"gqT{p}", "WdT"], writes=[PB[ds]],
                    name=f"dn{t}_{ds}_{ic}")
            s.add("dve", (lambda q, ds=ds, t=t: q.tensor_scalar(
                wd4[:, ds * 512:(ds + 1) * 512], bank(ds),
                adal[:, t:t + 1], None, ALU.mult)),
                reads=[PB[ds], "adal"], writes=["wd4_r"], name=f"ed{t}_{ds}")

    def tail4b(t):
        p = t % 2
        s.dma("sp", xt4b[:, p, :], xnew_scr[t * 128:(t + 1) * 128, :],
              reads=[f"xn.{t}"], writes=[f"xtb{p}"], name=f"ldxb{t}")
        s.add("dve", (lambda q, p=p: q.tensor_reduce(
            c3["s"][p], wd4, AX.X, ALU.add)),
            reads=["wd4_r"], writes=[f"c3s{p}"], name=f"l4s{t}")
        s.add("act", (lambda q, p=p: q.activation(
            dummy1k, wd4, AF.Square, accum_out=c3["q"][p])),
            reads=["wd4_r"], writes=["dummy1k"], accum=[f"c3q{p}"],
            name=f"l4q{t}")
        s.add("pool", (lambda q, p=p: q.tensor_scalar(
            c3["m"][p], c3["s"][p], 1.0 / C, None, ALU.mult)),
            reads=[f"c3s{p}"], writes=[f"c3m{p}"], name=f"l4m{t}")
        s.add("pool", (lambda q, p=p: q.tensor_tensor(
            c3["t1"][p], c3["m"][p], c3["m"][p], ALU.mult)),
            reads=[f"c3m{p}"], writes=[f"c3t1{p}"], name=f"l4mm{t}")
        s.add("pool", (lambda q, p=p: q.tensor_scalar(
            c3["t2"][p], c3["q"][p], 1.0 / C, EPS, ALU.mult, ALU.add)),
            reads=[f"c3q{p}"], writes=[f"c3t2{p}"], name=f"l4qq{t}")
        s.add("pool", (lambda q, p=p: q.tensor_tensor(
            c3["t2"][p], c3["t2"][p], c3["t1"][p], ALU.subtract)),
            reads=[f"c3t2{p}", f"c3t1{p}"], writes=[f"c3t2{p}"],
            name=f"l4v{t}")
        s.add("pool", (lambda q, p=p: q.tensor_tensor(
            c3["r"][p], c3["t2"][p], nh_col, ALU.pow)),
            reads=[f"c3t2{p}", "neghalf"], writes=[f"c3r{p}"], name=f"l4r{t}")
        s.add("pool", (lambda q, p=p: q.tensor_tensor(
            c3["t1"][p], c3["m"][p], c3["r"][p], ALU.mult)),
            reads=[f"c3m{p}", f"c3r{p}"], writes=[f"c3t1{p}"], name=f"l4mr{t}")
        s.add("dve", (lambda q, p=p: q.scalar_tensor_tensor(
            out=xo4, in0=wd4, scalar=c3["r"][p], in1=xt4b[:, p, :],
            op0=ALU.mult, op1=ALU.add)),
            reads=["wd4_r", f"c3r{p}", f"xtb{p}"], writes=["xo4_r"],
            name=f"rs4{t}")
        s.add("dve", (lambda q, p=p: q.tensor_scalar(
            xo4, xo4, c3["t1"][p], None, ALU.subtract)),
            reads=["xo4_r", f"c3t1{p}"], writes=["xo4_r"], name=f"rs5{t}")
        s.dma("sp", y_out[t * 128:(t + 1) * 128, :], xo4,
              reads=["xo4_r"], writes=[f"y.{t}"], name=f"sty{t}")

    s.dma("sp", gq4[:, 0, :], gq_scr[0:128, :],
          reads=["gqs.0"], writes=["gq40"], name="ldgq0")
    transp4b(0)
    for t in range(NTO):
        if t + 1 < NTO:
            s.dma("sp", gq4[:, (t + 1) % 2, :],
                  gq_scr[(t + 1) * 128:(t + 2) * 128, :],
                  reads=[f"gqs.{t + 1}"], writes=[f"gq4{(t + 1) % 2}"],
                  name=f"ldgq{t + 1}")
        mm4b(t)
        if t + 1 < NTO:
            transp4b(t + 1)
        tail4b(t)

    if taps:
        for fn in taps.values():
            fn(nc, s, locals())

    s.add("sp", lambda q: None, reads=[f"y.{t}" for t in range(NTO)],
          name="sink")
    s.build()
    return nc


_CACHE = {}


def kernel(L0_attn_x, L0_attn_W_attn, L0_attn_W_proj, L0_attn_lora_attn_A,
           L0_attn_lora_attn_B, L0_attn_lora_proj_A, L0_attn_lora_proj_B,
           L0_attn_bn_gamma, L0_attn_bn_beta, L0_ffn_w_gate, L0_ffn_w_up,
           L0_ffn_w_down):
    if "nc" not in _CACHE:
        _CACHE["nc"] = _build()
    nc = _CACHE["nc"]
    x = np.ascontiguousarray(np.asarray(L0_attn_x, dtype=np.float32))
    B = x.shape[0]
    common = {
        "W_attn": np.ascontiguousarray(L0_attn_W_attn, dtype=np.float32),
        "W_proj": np.ascontiguousarray(L0_attn_W_proj, dtype=np.float32),
        "lA": np.ascontiguousarray(L0_attn_lora_attn_A, dtype=np.float32),
        "lB": np.ascontiguousarray(L0_attn_lora_attn_B, dtype=np.float32),
        "pA": np.ascontiguousarray(L0_attn_lora_proj_A, dtype=np.float32),
        "pB": np.ascontiguousarray(L0_attn_lora_proj_B, dtype=np.float32),
        "gamma": np.ascontiguousarray(L0_attn_bn_gamma, dtype=np.float32),
        "beta": np.ascontiguousarray(L0_attn_bn_beta, dtype=np.float32),
        "w_gate": np.ascontiguousarray(L0_ffn_w_gate, dtype=np.float32),
        "w_up": np.ascontiguousarray(L0_ffn_w_up, dtype=np.float32),
        "w_down": np.ascontiguousarray(L0_ffn_w_down, dtype=np.float32),
    }
    in_maps = []
    for b in range(B):
        m = dict(common)
        m["x_all"] = np.ascontiguousarray(
            np.roll(x, -b, axis=0).reshape(B * T, C))
        in_maps.append(m)
    res = run_bass_kernel_spmd(nc, in_maps, core_ids=list(range(B)))
    out = np.stack([res.results[b]["y"] for b in range(B)], axis=0)
    return out.astype(np.float32)
